# version 06 (final)
# baseline (speedup 1.0000x reference)
import numpy as np
import ml_dtypes
from contextlib import ExitStack

import concourse.bass as bass
import concourse.mybir as mybir
from concourse.bass_utils import run_bass_kernel_spmd

F32 = mybir.dt.float32
BF16 = mybir.dt.bfloat16
I32 = mybir.dt.int32
AF = mybir.ActivationFunctionType
ALU = mybir.AluOpType
AX = mybir.AxisListType

D = 4096
S = 2048
IC = 32
NHL = 16
NKVL = 4
HD = 128
TOWN = 1024
SM_SCALE = float(HD) ** -0.5
SM_SHIFT = -40.0
EPS = 1e-5
E_SCALE = 1.0 / np.sqrt(32.0)

DEBUG = False

ENGS = ("sync", "gpsimd", "tensor", "vector", "scalar")
SEM_ENG = {"T": "tensor", "V": "vector", "SC": "scalar"}


class Sched:
    def __init__(self):
        self.q = {e: [] for e in ENGS}
        self.lines = {e: [] for e in ENGS}
        self.cnt = {}
        self.wm = {e: {} for e in ENGS}
        self.dma_sems = set()

    def op(self, eng, fn, waits=(), inc=None, n=1):
        import sys
        fr = sys._getframe(1)
        while fr.f_code.co_name in ("dma", "mm", "rd", "ln_finish_var"):
            fr = fr.f_back
        self.lines[eng].append(fr.f_lineno)
        flat = []

        def _fl(t):
            if t is None:
                return
            if isinstance(t, tuple) and t and isinstance(t[0], str):
                flat.append(t)
            elif isinstance(t, (list, tuple)):
                for u in t:
                    _fl(u)
            else:
                flat.append(t)
        for t in waits:
            _fl(t)
        wl = {}
        for t in flat:
            if t is None:
                continue
            s, v = t
            if SEM_ENG.get(s) == eng:
                continue
            if v > wl.get(s, -1):
                wl[s] = v
        wl2 = []
        for s, v in wl.items():
            if v <= self.wm[eng].get(s, 0):
                continue
            self.wm[eng][s] = v
            wl2.append((s, v))
        tok = None
        if inc is not None:
            self.cnt[inc] = self.cnt.get(inc, 0) + n
            tok = (inc, self.cnt[inc])
        self.q[eng].append((wl2, fn, inc, n))
        return tok

    def dma(self, eng, out_ap, in_ap, waits=(), sem=None):
        assert sem is not None, "every DMA needs a dedicated slot semaphore"
        self.dma_sems.add(sem)
        return self.op(eng, lambda e, o=out_ap, i=in_ap: e.dma_start(o, i),
                       waits, inc=sem, n=16)


class Bank:

    def __init__(self, t):
        self.t = t
        self.readers = {}
        self.wtok = None

    def rtoks(self):
        return list(self.readers.items())

    def add_reader(self, tok):
        if tok[1] > self.readers.get(tok[0], -1):
            self.readers[tok[0]] = tok[1]


def build_nc():
    nc = bass.Bass("TRN2", target_bir_lowering=False)

    def din(name, shape, dt):
        return nc.dram_tensor(name, list(shape), dt, kind="ExternalInput").ap()

    xT = din("xT", (D, S), F32)
    xTo = din("xTo", (D, TOWN), F32)
    qcod = din("qcod", (4, D, 512), I32)
    kcod = din("kcod", (D, 512), I32)
    vcod = din("vcod", (D, 512), I32)
    ocod = din("ocod", (16, 2048, 256), I32)
    qsc = din("qsc", (32, 2048), F32)
    ksc = din("ksc", (32, 512), F32)
    vsc = din("vsc", (32, 512), F32)
    osc = din("osc", (16, D), F32)
    WT = din("WT", (D, D), F32)
    WqfT = din("WqfT", (D, 256), F32)
    WkfT = din("WkfT", (D, 256), F32)
    ATt = din("ATt", (D, 128), F32)
    BmTt = din("BmTt", (128, D), F32)
    bh = din("bh", (128, 32), F32)
    avgb = din("avgb", (128, 128), BF16)
    onesb = din("onesb", (128, 128), BF16)
    identf = din("identf", (128, 128), F32)
    epermb = din("epermb", (128, 128), BF16)
    maskTb = din("maskTb", (128, 128), BF16)
    maskqb = din("maskqb", (128, 128), BF16)
    seg0 = din("seg0", (128, 8), BF16)
    seg1 = din("seg1", (128, 8), BF16)
    exp8f = din("exp8f", (8, 128), BF16)
    constv = din("constv", (128, 12), F32)
    selb = din("selb", (32, D), BF16)
    cosf = din("cosf", (128, S), F32)
    sinf = din("sinf", (128, S), F32)

    out_t = nc.dram_tensor("outT", [D, TOWN], F32, kind="ExternalOutput")

    ln1sp_t = nc.dram_tensor("ln1sp", [4, 128, IC * 512], BF16)
    ln1sp = ln1sp_t.ap()
    rs_in = nc.dram_tensor("rs_in", [2, 32, 128, TOWN], BF16).ap()
    rs_out = nc.dram_tensor("rs_out", [32, 128, TOWN], BF16).ap()
    ysp = nc.dram_tensor("ysp", [32, 128, TOWN], BF16).ap()
    zsp = nc.dram_tensor("zsp", [32, 128, TOWN], BF16).ap()
    outtsp = nc.dram_tensor("outtsp", [32, 128, TOWN], BF16).ap()

    dbg = {}
    if DEBUG:
        for name, shape, dt in [
            ("dbg_ln1", (4, 128, IC * 512), BF16), ("dbg_kt", (128, 8192), BF16),
            ("dbg_v", (128, 8192), BF16), ("dbg_qt", (128, NHL * S), BF16),
            ("dbg_ag", (128, NHL * S), BF16),
            ("dbg_outt", (32, 128, TOWN), BF16), ("dbg_y", (32, 128, TOWN), BF16),
            ("dbg_ln2", (128, IC * TOWN), BF16), ("dbg_z", (32, 128, TOWN), BF16),
            ("dbg_wk", (128, 16384), BF16), ("dbg_wv", (128, 16384), BF16),
            ("dbg_qf", (128, 2048), BF16), ("dbg_kf", (128, 2048), BF16),
            ("dbg_swp", (128, 1024), BF16), ("dbg_krb", (128, 2048), BF16),
            ("dbg_sn", (128, 64), F32), ("dbg_w8", (8, 512), F32),
            ("dbg_sp", (128, 2048), BF16),
            ("dbg_wexp", (128, 1024), BF16), ("dbg_g", (128, 1024), BF16),
        ]:
            dbg[name] = nc.dram_tensor(name, list(shape), dt, kind="ExternalOutput")

    st = ExitStack()
    sb = lambda name, shp, dt: st.enter_context(nc.sbuf_tensor(name, shp, dt))
    psum = lambda name: st.enter_context(nc.psum_tensor(name, [128, 512], F32))

    reg1 = sb("reg1", [128, 65536], mybir.dt.uint8)
    reg2 = sb("reg2", [128, 65536], mybir.dt.uint8)
    reg3 = sb("reg3", [128, 49152], mybir.dt.uint8)

    def view(region, off_bytes, name, shape, dt):
        base = nc.lookup_mloc(region).addr
        return nc.alloc_sbuf_tensor_at(name, list(shape), dt, offset=base + off_bytes)

    wk = view(reg1, 0, "wk", [128, 16384], BF16)
    wv = view(reg1, 32768, "wv", [128, 16384], BF16)
    wq = view(reg1, 0, "wq", [128, 16384], BF16)
    qta = view(reg1, 32768, "qta", [128, 16384], BF16)
    otl = view(reg1, 0, "otl", [128, 32768], BF16)
    qf = view(reg1, 0, "qf", [128, 2048], BF16)
    kf = view(reg1, 4096, "kf", [128, 2048], BF16)
    qk = view(reg1, 8192, "qk", [128, 2048], BF16)
    w8 = view(reg1, 12288, "w8", [8, 512], BF16)
    sce = view(reg1, 14336, "sce", [8, 512], F32)
    snat = view(reg1, 16384, "snat", [128, 32], F32)
    wn = view(reg1, 16512, "wn", [128, 32], F32)
    wexp = view(reg1, 16640, "wexp", [128, 1024], BF16)
    gsb = view(reg1, 18688, "gsb", [128, 1024], BF16)
    outt = view(reg1, 0, "outt", [128, 32768], BF16)
    cosfs = view(reg1, 0, "cosfs", [128, S], F32)
    sinfs = view(reg1, 8192, "sinfs", [128, S], F32)
    xbf = view(reg2, 0, "xbf", [128, 32768], BF16)
    rr = view(reg2, 0, "rr", [128, 16384], BF16)
    qtb = view(reg2, 32768, "qtb", [128, 16384], BF16)
    pt = view(reg2, 0, "pt", [128, 16384], BF16)
    ocod_sb = view(reg2, 0, "ocod_sb", [128, 8192], I32)
    ln2 = view(reg2, 0, "ln2", [128, 32768], BF16)
    kt = view(reg3, 0, "kt", [128, 8192], BF16)
    vsb = view(reg3, 16384, "vsb", [128, 8192], BF16)
    cosb = view(reg3, 32768, "cosb", [128, S], BF16)
    sinb = view(reg3, 36864, "sinb", [128, S], BF16)
    selv = view(reg3, 40960, "selv", [32, 4096], BF16)
    wo = view(reg3, 0, "wo", [128, 8192], BF16)
    wqfb = view(reg3, 0, "wqfb", [128, 8192], BF16)
    wkfb = view(reg3, 16384, "wkfb", [128, 8192], BF16)
    atb = view(reg3, 32768, "atb", [128, 4096], BF16)
    bmtb = view(reg3, 40960, "bmtb", [128, 4096], BF16)
    wblk = view(reg3, 0, "wblk", [128, 16384], BF16)
    wblk2 = view(reg1, 24576, "wblk2", [128, 16384], BF16)

    c_avg = sb("c_avg", [128, 128], BF16)
    c_ones = sb("c_ones", [128, 128], BF16)
    c_idf = sb("c_idf", [128, 128], F32)
    c_perm = sb("c_perm", [128, 128], BF16)
    c_mT = sb("c_mT", [128, 128], BF16)
    c_mq = sb("c_mq", [128, 128], BF16)
    c_seg0 = sb("c_seg0", [128, 8], BF16)
    c_seg1 = sb("c_seg1", [128, 8], BF16)
    c_e8 = sb("c_e8", [8, 128], BF16)
    c_cv = sb("c_cv", [128, 12], F32)
    c_b = sb("c_b", [128, 32], F32)
    ctile = sb("ctile", [128, 1024], I32)
    scsl = sb("scsl", [32, 512], F32)
    scbf = sb("scbf", [32, 512], BF16)
    m_sb = sb("m_sb", [128, 512], F32)
    var_sb = sb("var_sb", [128, 512], F32)
    rb_sb = sb("rb_sb", [128, 1024], F32)
    bstg = sb("bstg", [128, 2048], BF16)
    fstg = sb("fstg", [128, 1024], F32)
    pq = sb("pq", [128, 128], F32)
    trsc = sb("trsc", [128, 512], F32)
    tiny = sb("tiny", [128, 8], F32)

    pA, pB, pC, pD, pE, pF, pG, pH = (Bank(psum(n)) for n in
                                      ("pA", "pB", "pC", "pD", "pE", "pF", "pG", "pH"))

    sch = Sched()
    APc = bass.AP
    lnf = {"chain": None}

    def mm(bank, lhsT, rhs, start, stop, waits=(), out=None, sig=False):
        o = out if out is not None else bank.t[:]
        w = list(waits)
        if start:
            w += bank.rtoks()
        tok = sch.op("tensor",
                     lambda e, o=o, l=lhsT, r=rhs, s1=start, s2=stop:
                     e.matmul(o, l, r, start=s1, stop=s2),
                     w, inc=("T" if (stop or sig) else None))
        if start:
            bank.readers = {}
        if stop:
            bank.wtok = tok
        return tok

    def rd(eng, bank, fn, waits=()):
        sem = {"vector": "V", "scalar": "SC"}[eng]
        tok = sch.op(eng, fn, list(waits) + [bank.wtok], inc=sem)
        bank.add_reader(tok)
        return tok

    for dst, src in [(c_avg, avgb), (c_ones, onesb), (c_idf, identf),
                     (c_perm, epermb), (c_mT, maskTb), (c_mq, maskqb),
                     (c_seg0, seg0), (c_seg1, seg1), (c_e8, exp8f),
                     (c_cv, constv), (c_b, bh)]:
        sch.dma("sync", dst[:], src, sem="C")
    sch.dma("sync", cosfs[:], cosf, sem="C")
    sch.dma("sync", sinfs[:], sinf, sem="C")
    tok_c = sch.dma("sync", selv[:], selb, sem="C")
    t_cosb = sch.op("scalar", lambda e: e.copy(cosb[:], cosfs[:]), [tok_c], inc="SC")
    t_sinb = sch.op("scalar", lambda e: e.copy(sinb[:], sinfs[:]), [tok_c], inc="SC")

    sc_state = {"dma": None, "copy": None}

    def load_scales(src_ap, width, rows=32):
        t_sc = sch.dma("sync", scsl[0:rows, 0:width], src_ap,
                       waits=[sc_state["copy"]], sem="SCL")
        t_cp = sch.op("scalar",
                      lambda e, w=width, r=rows: e.copy(scbf[0:r, 0:w], scsl[0:r, 0:w]),
                      [t_sc] + pE.rtoks() + ([pE.wtok] if pE.wtok else []), inc="SC")
        sc_state["dma"] = t_sc
        sc_state["copy"] = t_cp
        return t_cp

    def bcast(ic, sc_tok, width, rows=32):
        return mm(pE, selv[0:rows, ic * 128:(ic + 1) * 128], scbf[0:rows, 0:width],
                  True, True, waits=[sc_tok, tok_c], out=pE.t[:, 0:width])

    def dequant(wdst_ap, cod_tok, ctile_ap, width, extra=()):
        tok = sch.op("vector",
                     lambda e, o=wdst_ap, c=ctile_ap, p=pE.t[:, 0:width]:
                     e.scalar_tensor_tensor(o, c, -8.0, p, ALU.add, ALU.mult),
                     list(extra) + [cod_tok, pE.wtok], inc="V")
        pE.add_reader(tok)
        return tok

    ct_free = [None, None]
    kv_deq_tok = None
    first_deq_guard = [t_cosb, t_sinb]
    for (wdst, cod, scv) in ((wk, kcod, ksc), (wv, vcod, vsc)):
        t_scb = load_scales(scv, 512)
        for ic in range(IC):
            slot = ic % 2
            ct = ctile[:, slot * 512:(slot + 1) * 512]
            t_cod = sch.dma("sync", ct, cod[ic * 128:(ic + 1) * 128, :],
                            waits=[ct_free[slot]], sem=f"CT{slot}")
            bcast(ic, t_scb, 512)
            kv_deq_tok = dequant(wdst[:, ic * 512:(ic + 1) * 512], t_cod, ct, 512,
                                 extra=first_deq_guard)
            first_deq_guard = []
            ct_free[slot] = kv_deq_tok

    dbg_guard = []
    if DEBUG:
        sch.dma("gpsimd", dbg["dbg_wk"].ap(), wk[:], waits=[kv_deq_tok], sem="DBG")
        t2_ = sch.dma("gpsimd", dbg["dbg_wv"].ap(), wv[:], waits=[kv_deq_tok],
                      sem="DBG")
        dbg_guard = [t2_]
    xslot_free = [[], []]
    sp_toks = [None] * 4
    ln1_tok = [None] * 4
    ktok = {}
    vtok = {}
    sq_free = [None] * 4
    rope_scr = [None, None]
    for ch in range(4):
        slot = ch % 2
        xoff = slot * 16384
        dx = []
        for g8 in range(4):
            inap = APc(xT.tensor, (g8 * 8 * 128) * S + ch * 512,
                       [[S, 128], [128 * S, 8], [1, 512]])
            outap = APc(xbf, xoff + g8 * 8 * 512, [[32768, 128], [512, 8], [1, 512]])
            dx.append(sch.dma("gpsimd", outap, inap, waits=xslot_free[slot],
                              sem=f"X{slot}{g8}"))
        for ic in range(IC):
            bslot = ic % 4
            xs = APc(xbf, xoff + ic * 512, [[32768, 128], [1, 512]])
            sqout = bstg[:, bslot * 512:(bslot + 1) * 512]
            tsq = sch.op("scalar",
                         lambda e, o=sqout, i=xs: e.activation(o, i, AF.Square),
                         [dx[ic // 8], sq_free[bslot]], inc="SC")
            mm(pA, c_avg[:], xs, ic == 0, ic == IC - 1, waits=[dx[ic // 8]])
            tb = mm(pB, c_avg[:], sqout, ic == 0, ic == IC - 1, waits=[tsq], sig=True)
            sq_free[bslot] = tb
        t_m = rd("scalar", pA, lambda e: e.copy(m_sb[:], pA.t[:]), waits=[lnf["chain"]])
        t_ms = sch.op("vector", lambda e: e.tensor_mul(var_sb[:], m_sb[:], m_sb[:]),
                      [t_m], inc="V")
        lnf["chain"] = t_ms
        t_var = rd("vector", pB, lambda e: e.tensor_sub(var_sb[:], pB.t[:], var_sb[:]),
                   [t_ms])
        rbs = rb_sb[:, slot * 512:slot * 512 + 512]
        t_ln = sch.op("scalar", lambda e, o=rbs: e.activation(o, var_sb[:], AF.Ln,
                                                              bias=c_cv[:, 2:3]),
                      [t_var], inc="SC")
        t_rb = sch.op("scalar", lambda e, o=rbs: e.activation(o, o, AF.Exp,
                                                              scale=c_cv[:, 5:6]),
                      [t_ln], inc="SC")
        x3 = APc(xbf, xoff, [[32768, 128], [512, 32], [1, 512]])
        mbc = APc(pA.t, 0, [[512, 128], [0, 32], [1, 512]])
        t_ns = rd("vector", pA, lambda e, a=x3, b=mbc: e.tensor_sub(a, a, b))
        rbc = APc(rb_sb, slot * 512, [[1024, 128], [0, 32], [1, 512]])
        t_nm = sch.op("vector", lambda e, a=x3, b=rbc: e.tensor_mul(a, a, b),
                      [t_rb, t_ns], inc="V")
        ln1_tok[ch] = t_nm
        t_sp = sch.dma("gpsimd", ln1sp[ch], APc(xbf, xoff, [[32768, 128], [1, 16384]]),
                       waits=[t_nm], sem=f"LW{slot}")
        last_kv = None
        for kvh in range(NKVL):
            bank = pC if kvh % 2 == 0 else pD
            for ic in range(IC):
                last_kv = mm(bank,
                             wk[:, ic * 512 + kvh * 128: ic * 512 + kvh * 128 + 128],
                             APc(xbf, xoff + ic * 512, [[32768, 128], [1, 512]]),
                             ic == 0, ic == IC - 1, waits=[t_nm, kv_deq_tok])
            rs = kvh % 2
            krb = bstg[:, rs * 512:(rs + 1) * 512]
            t_ev = rd("scalar", bank, lambda e, o=krb, b=bank: e.copy(o, b.t[:]),
                      waits=[rope_scr[rs], sq_free[rs]])
            t_pm = mm(pF, c_perm[:], krb, True, True, waits=[t_ev])
            t1s = bstg[:, (2 + rs) * 512:(3 + rs) * 512]
            cs = cosb[:, ch * 512:(ch + 1) * 512]
            sn = sinb[:, ch * 512:(ch + 1) * 512]
            t_t1 = sch.op("vector", lambda e, o=t1s, a=krb, c=cs: e.tensor_mul(o, a, c),
                          [t_ev, t_cosb, sq_free[2 + rs]], inc="V")
            ktd = kt[:, kvh * 2048 + ch * 512: kvh * 2048 + ch * 512 + 512]
            t_t2 = rd("vector", pF, lambda e, o=ktd, b=sn: e.tensor_mul(o, pF.t[:], b),
                      waits=[t_sinb])
            t_t3 = sch.op("vector", lambda e, o=ktd, a=t1s: e.tensor_add(o, o, a),
                          [t_t1, t_t2], inc="V")
            rope_scr[rs] = t_t3
            sq_free[rs] = [t_t1, t_pm]
            sq_free[2 + rs] = t_t3
            ktok[(kvh, ch)] = t_t3
        for ts in range(4):
            bank = pC if ts % 2 == 0 else pD
            for ic in range(IC):
                last_kv = mm(bank,
                             APc(xbf, xoff + ic * 512 + ts * 128, [[32768, 128], [1, 128]]),
                             wv[:, ic * 512:(ic + 1) * 512],
                             ic == 0, ic == IC - 1, waits=[t_nm, kv_deq_tok])
            kc = ch * 4 + ts
            t_ev = rd("scalar", bank,
                      lambda e, o=vsb[:, kc * 512:(kc + 1) * 512], b=bank: e.copy(o, b.t[:]))
            vtok[kc] = t_ev
        xslot_free[slot] = [last_kv, t_sp]
        sp_toks[ch] = t_sp

    if DEBUG:
        for ch in range(4):
            sch.dma("gpsimd", dbg["dbg_ln1"].ap()[ch], ln1sp[ch],
                    waits=[sp_toks[ch]], sem="DBH")
        sch.dma("gpsimd", dbg["dbg_kt"].ap(), kt[:], waits=[ktok[(NKVL - 1, 3)]],
                sem="DBH")
        sch.dma("gpsimd", dbg["dbg_v"].ap(), vsb[:], waits=[vtok[15]], sem="DBH")

    qt_tok = {}
    rr_free = [list(xslot_free[0]) + list(xslot_free[1]),
               list(xslot_free[0]) + list(xslot_free[1])]
    qtb_guard = list(xslot_free[1])
    for hg in range(4):
        t_scb = load_scales(qsc[:, hg * 512:(hg + 1) * 512], 512)
        q_deq_tok = None
        for ic in range(IC):
            slot = ic % 2
            ct = ctile[:, slot * 512:(slot + 1) * 512]
            t_cod = sch.dma("sync", ct, qcod[hg][ic * 128:(ic + 1) * 128, :],
                            waits=[ct_free[slot]], sem=f"CT{slot}")
            bcast(ic, t_scb, 512)
            q_deq_tok = dequant(wq[:, ic * 512:(ic + 1) * 512], t_cod, ct, 512,
                                extra=(([xslot_free[0][0], xslot_free[1][0]] + dbg_guard)
                                       if (hg == 0 and ic < 2) else ()))
            ct_free[slot] = q_deq_tok
        for ch in range(4):
            h_tok = [None, None]
            for half in range(2):
                inap = APc(ln1sp_t, (ch * 128 * 16384) + half * 8192,
                           [[16384, 128], [512, 16], [1, 512]])
                outap = APc(rr, half * 8192, [[16384, 128], [512, 16], [1, 512]])
                h_tok[half] = sch.dma("gpsimd", outap, inap,
                                      waits=rr_free[half] + [sp_toks[ch]],
                                      sem=f"RR{half}")
            last_g = None
            for hl in range(4):
                h = hg * 4 + hl
                bank = pC if hl % 2 == 0 else pD
                for ic in range(IC):
                    half = ic // 16
                    icr = ic % 16
                    last_g = mm(bank,
                                wq[:, ic * 512 + hl * 128: ic * 512 + hl * 128 + 128],
                                APc(rr, half * 8192 + icr * 512, [[16384, 128], [1, 512]]),
                                ic == 0, ic == IC - 1, waits=[q_deq_tok, h_tok[half]])
                rs = hl % 2
                qrb = bstg[:, rs * 512:(rs + 1) * 512]
                t_ev = rd("scalar", bank, lambda e, o=qrb, b=bank: e.copy(o, b.t[:]),
                          waits=[rope_scr[rs], sq_free[rs]])
                t_pm = mm(pF, c_perm[:], qrb, True, True, waits=[t_ev])
                t1s = bstg[:, (2 + rs) * 512:(3 + rs) * 512]
                cs = cosb[:, ch * 512:(ch + 1) * 512]
                sn = sinb[:, ch * 512:(ch + 1) * 512]
                t_t1 = sch.op("vector",
                              lambda e, o=t1s, a=qrb, c=cs: e.tensor_mul(o, a, c),
                              [t_ev, t_cosb, sq_free[2 + rs]], inc="V")
                if h < 8:
                    qdst = qta[:, h * 2048 + ch * 512: h * 2048 + ch * 512 + 512]
                else:
                    qdst = qtb[:, (h - 8) * 2048 + ch * 512: (h - 8) * 2048 + ch * 512 + 512]
                t_t2 = rd("vector", pF,
                          lambda e, o=qdst, b=sn: e.tensor_mul(o, pF.t[:], b),
                          waits=[t_sinb] + (list(qtb_guard) if h >= 8 else []))
                t_t3 = sch.op("vector", lambda e, o=qdst, a=t1s: e.tensor_add(o, o, a),
                              [t_t1, t_t2], inc="V")
                rope_scr[rs] = t_t3
                sq_free[rs] = [t_t1, t_pm]
                sq_free[2 + rs] = t_t3
                qt_tok[(h, ch)] = t_t3
            rr_free[0] = [last_g]
            rr_free[1] = [last_g]
            q_last_mm = last_g

    if DEBUG:
        sch.dma("gpsimd", dbg["dbg_qt"].ap()[:, 0:16384], qta[:],
                waits=[qt_tok[(7, 3)]], sem="DBG")
        t6_ = sch.dma("gpsimd", APc(dbg["dbg_qt"], 16384, [[NHL * S, 128], [1, 16384]]),
                      qtb[:], waits=[qt_tok[(15, 3)]], sem="DBG")
        qtb_guard = qtb_guard + [t6_]
        q_last_mm = q_last_mm

    sp_pt_tok = {}
    cps_free = [None] * 4
    pnb_free = [None] * 4
    trsc_free = [None] * 4
    bstg_guard = [rope_scr[0], rope_scr[1], sq_free[0], sq_free[1],
                  sq_free[2], sq_free[3]]
    fstg_gate0 = fstg_gate1 = None
    for g0 in range(0, NHL, 4):
        grp = list(range(g0, g0 + 4))
        t_nat = {}
        t_cp = {}
        t_mq = {}
        t_st = {}
        t_neg = {}
        t_exp = {}
        for h in grp:
            sl4 = h % 4
            qb0 = (qta[:, h * 2048: h * 2048 + 128] if h < 8
                   else qtb[:, (h - 8) * 2048: (h - 8) * 2048 + 128])
            t_nat[h] = mm(pE, qb0, kt[:, (h // 4) * 2048: (h // 4) * 2048 + 128],
                          sl4 == 0, True,
                          waits=[qt_tok[(h, 0)], ktok[(h // 4, 0)]],
                          out=pE.t[:, sl4 * 128:(sl4 + 1) * 128])
        for h in grp:
            sl4 = h % 4
            t_cp[h] = sch.op("scalar",
                             lambda e, o=fstg[:, (h % 4) * 128:(h % 4) * 128 + 128],
                             i=pE.t[:, sl4 * 128:(sl4 + 1) * 128]: e.copy(o, i),
                             [t_nat[h], cps_free[sl4]], inc="SC")
            pE.add_reader(t_cp[h])
        for h in grp:
            t_mq[h] = sch.op("vector",
                             lambda e, o=fstg[:, (h % 4) * 128:(h % 4) * 128 + 128]:
                             e.tensor_add(o, o, c_mq[:]),
                             [t_cp[h], tok_c], inc="V")
        for h in grp:
            sl4 = h % 4
            cw = fstg[:, sl4 * 128:sl4 * 128 + 128]
            tw = trsc[:, sl4 * 128:sl4 * 128 + 128]
            t_st[h] = sch.op("vector",
                             lambda e, o=tw, i=cw:
                             e.tensor_tensor(o[:, 0:64], i[:, 0:64],
                                             i[:, 64:128], ALU.max),
                             [t_mq[h], trsc_free[sl4]], inc="V")
        wdt, off = 32, 0
        while wdt >= 1:
            for h in grp:
                tw = trsc[:, (h % 4) * 128:(h % 4) * 128 + 128]
                t_st[h] = sch.op("vector",
                                 lambda e, o=tw, wd=wdt, of=off:
                                 e.tensor_tensor(o[:, of + 2 * wd:of + 3 * wd],
                                                 o[:, of:of + wd],
                                                 o[:, of + wd:of + 2 * wd],
                                                 ALU.max),
                                 [t_st[h]], inc="V")
            off += 2 * wdt
            wdt //= 2
        for h in grp:
            tw = trsc[:, (h % 4) * 128:(h % 4) * 128 + 128]
            t_neg[h] = sch.op("vector",
                              lambda e, o=tw:
                              e.tensor_scalar_mul(o[:, 127:128], o[:, 126:127],
                                                  c_cv[:, 9:10]),
                              [t_st[h]], inc="V")
        for h in grp:
            sl4 = h % 4
            t_exp[h] = sch.op("scalar",
                              lambda e, o=fstg[:, 512 + sl4 * 128:640 + sl4 * 128],
                              i=fstg[:, sl4 * 128:sl4 * 128 + 128],
                              b=trsc[:, sl4 * 128 + 127:sl4 * 128 + 128]:
                              e.activation(o, i, AF.Exp, bias=b,
                                           scale=c_cv[:, 1:2]),
                              [t_neg[h], pnb_free[sl4]], inc="SC")
            cps_free[sl4] = t_exp[h]
            trsc_free[sl4] = t_exp[h]
        for h in grp:
            sl4 = h % 4
            w_tr = [t_exp[h]] + pF.rtoks()
            t_tr = sch.op("tensor",
                          lambda e, i=fstg[:, 512 + sl4 * 128:640 + sl4 * 128]:
                          e.transpose(pF.t[0:128, 0:128], i, c_idf[:]),
                          w_tr, inc="T")
            pF.readers = {}
            pF.wtok = t_tr
            pnb_free[sl4] = t_tr
            sp_pt_tok[h] = rd("scalar", pF,
                              lambda e, o=bstg[:, h * 128:(h + 1) * 128]:
                              e.copy(o, pF.t[0:128, 0:128]),
                              waits=bstg_guard)
        if g0 == 12:
            fstg_gate0 = [t_exp[h] for h in grp]
            fstg_gate1 = [pnb_free[k] for k in range(4)]
    if DEBUG:
        t_dsp = sch.dma("gpsimd", dbg["dbg_sp"].ap(), bstg[:],
                        waits=[sp_pt_tok[h] for h in range(NHL)], sem="DBH")

    pt_free = [q_last_mm, q_last_mm]
    otl_tok = {}
    qta_guard = []
    cpy_tok = {}
    for h in range(NHL):
        kvh = h // 4
        for w in range(4):
            if h < 8:
                qb = qta[:, h * 2048 + w * 512: h * 2048 + w * 512 + 512]
            else:
                qb = qtb[:, (h - 8) * 2048 + w * 512: (h - 8) * 2048 + w * 512 + 512]
            it = h * 4 + w
            p = it % 2
            pbase = p * 8192
            sums = pA if p == 0 else pB
            ob = pG if p == 0 else pH
            KCW = 4 * (w + 1)
            qrdy = qt_tok[(h, w)]
            deferred = []

            def emit_cons(kc, toks, KCW=KCW, sums=sums, ob=ob, pbase=pbase, kvh=kvh):
                ptsl = APc(pt, pbase + kc * 512, [[16384, 128], [1, 512]])
                mm(sums, c_ones[:], ptsl, kc == 0, kc == KCW - 1, waits=toks)
                mm(ob, vsb[:, kc * 512 + kvh * 128: kc * 512 + kvh * 128 + 128], ptsl,
                   kc == 0, kc == KCW - 1, waits=toks + [vtok[kc]])

            for kc in range(KCW):
                bank = pC if kc % 2 == 0 else pD
                t_ms = mm(bank, kt[:, kvh * 2048 + kc * 128: kvh * 2048 + kc * 128 + 128],
                          qb, True, True,
                          waits=[qrdy, ktok[(kvh, kc // 4)], pt_free[p]])
                jstar = kc - 4 * w
                toks = []
                if jstar < 0:
                    toks.append(rd("scalar", bank,
                                   lambda e, b=bank,
                                   o=APc(pt, pbase + kc * 512, [[16384, 128], [1, 512]]):
                                   e.activation(o, b.t[:], AF.Exp, bias=c_cv[:, 0:1],
                                                 scale=c_cv[:, 1:2])))
                else:
                    if jstar > 0:
                        toks.append(sch.op("vector",
                                           lambda e, o=APc(pt, pbase + kc * 512,
                                                           [[16384, 128], [1, jstar * 128]]):
                                           e.memset(o, 0.0), [pt_free[p]], inc="V"))
                    if w == 0 and kc == 0:
                        tok_e = sch.op("scalar",
                                       lambda e,
                                       o=APc(pt, pbase, [[16384, 128], [1, 128]]),
                                       i=bstg[:, h * 128:(h + 1) * 128]:
                                       e.copy(o, i),
                                       [sp_pt_tok[h], pt_free[p]], inc="SC")
                        cpy_tok[h] = tok_e
                        toks.append(tok_e)
                        toks.append(rd("scalar", bank,
                                       lambda e, b=bank,
                                       o=APc(pt, pbase + 128, [[16384, 128], [1, 384]]):
                                       e.activation(o, b.t[:, 128:512], AF.Exp,
                                                     bias=c_cv[:, 0:1], scale=c_cv[:, 1:2])))
                    else:
                        t_madd = sch.op("vector",
                                        lambda e, b=bank, j=jstar:
                                        e.tensor_add(b.t[:, j * 128:(j + 1) * 128],
                                                     b.t[:, j * 128:(j + 1) * 128], c_mT[:]),
                                        [t_ms], inc="V")
                        bank.add_reader(t_madd)
                        toks.append(rd("scalar", bank,
                                       lambda e, b=bank, j=jstar,
                                       o=APc(pt, pbase + kc * 512 + jstar * 128,
                                             [[16384, 128], [1, (4 - jstar) * 128]]):
                                       e.activation(o, b.t[:, j * 128:512], AF.Exp,
                                                     bias=c_cv[:, 0:1], scale=c_cv[:, 1:2]),
                                       waits=[t_madd]))
                deferred.append((kc, toks))
                if len(deferred) > 1:
                    kcp, tks = deferred.pop(0)
                    emit_cons(kcp, tks)
            while deferred:
                kcp, tks = deferred.pop(0)
                emit_cons(kcp, tks)
            t_l1 = rd("scalar", sums, lambda e, s=sums: e.activation(m_sb[:], s.t[:],
                                                                     AF.Ln,
                                                                     scale=c_cv[:, 7:8]))
            t_rc = sch.op("scalar", lambda e: e.activation(m_sb[:], m_sb[:], AF.Exp,
                                                           scale=c_cv[:, 6:7],
                                                           bias=c_cv[:, 8:9]),
                          [t_l1], inc="SC")
            odst = APc(otl, h * 2048 + w * 512, [[32768, 128], [1, 512]])
            guard = []
            if h == 0 and w == 0:
                guard = [q_last_mm] + (dbg_guard if DEBUG else [])
            if h == 8 and w == 0:
                guard = list(qta_guard) + (qtb_guard[2:] if DEBUG else [])
            t_oe = rd("vector", ob,
                      lambda e, o=odst, b=ob: e.tensor_mul(o, b.t[:], m_sb[:]),
                      waits=[t_rc] + guard)
            otl_tok[(h, w)] = t_oe
            pt_free[p] = ob.wtok
        if h == 7:
            qta_guard = [pt_free[0], pt_free[1]]

    if DEBUG:
        dbg_ag_tok = sch.dma("gpsimd", dbg["dbg_ag"].ap(), otl[:],
                             waits=[otl_tok[(15, 3)]], sem="DBH")

    ocod_free = [[pt_free[0], pt_free[1]], [pt_free[0], pt_free[1]]]
    wo_free = [[pt_free[0], pt_free[1]] + dbg_guard,
               [pt_free[0], pt_free[1]] + dbg_guard]
    sq_free = [[rope_scr[0]] + [cpy_tok[j] for j in range(0, 4)],
               [rope_scr[1]] + [cpy_tok[j] for j in range(4, 8)],
               [sq_free[2]] + [cpy_tok[j] for j in range(8, 12)],
               [sq_free[3]] + [cpy_tok[j] for j in range(12, 16)]]
    if DEBUG:
        sq_free = [sq_free[k] + [t_dsp] for k in range(4)]
    rsin_tok = [None] * 4
    for mb in range(16):
        bs = mb % 2
        cd = []
        for g8 in range(2):
            inap = APc(ocod.tensor, mb * 2048 * 256 + (g8 * 8 * 128) * 256,
                       [[256, 128], [128 * 256, 8], [1, 256]])
            outap = APc(ocod_sb, bs * 4096 + g8 * 8 * 256,
                        [[8192, 128], [256, 8], [1, 256]])
            cd.append(sch.dma("sync", outap, inap, waits=ocod_free[bs],
                               sem=f"OC{bs}"))
        t_scb = load_scales(osc[:, mb * 256:(mb + 1) * 256], 256, rows=16)
        dq = None
        for icl in range(16):
            bcast(icl, t_scb, 256, rows=16)
            dq = dequant(wo[:, bs * 4096 + icl * 256: bs * 4096 + icl * 256 + 256],
                         cd[1],
                         APc(ocod_sb, bs * 4096 + icl * 256, [[8192, 128], [1, 256]]),
                         256, extra=wo_free[bs])
        ocod_free[bs] = [dq]
        last_g = None
        for mo in range(2):
            m = mb * 2 + mo
            for tcg in range(4):
                bank = pC if (mo * 4 + tcg) % 2 == 0 else pD
                for icl in range(16):
                    last_g = mm(bank,
                                APc(wo, bs * 4096 + icl * 256 + mo * 128,
                                    [[8192, 128], [1, 128]]),
                                APc(otl, icl * 2048 + tcg * 512, [[32768, 128], [1, 512]]),
                                icl == 0, icl == 15,
                                waits=[dq, otl_tok[(icl, tcg)]])
                sl = (m * 4 + tcg) % 4
                ost = bstg[:, sl * 512:(sl + 1) * 512]
                t_ev = rd("scalar", bank, lambda e, o=ost, b=bank: e.copy(o, b.t[:]),
                          waits=[sq_free[sl]])
                tok = sch.dma("gpsimd",
                              rs_in[tcg // 2][m][:, (tcg % 2) * 512:(tcg % 2) * 512 + 512],
                              ost, waits=[t_ev], sem=f"RSI{sl}")
                sq_free[sl] = tok
                rsin_tok[sl] = tok
        wo_free[bs] = [last_g]

    t_cc = sch.op("gpsimd",
                  lambda e: e.collective_compute(
                      "ReduceScatter", ALU.add,
                      replica_groups=[[0, 1], [2, 3], [4, 5], [6, 7]],
                      ins=[rs_in.opt()], outs=[rs_out.opt()]),
                  waits=list(rsin_tok), inc="CC")
    if DEBUG:
        sch.dma("gpsimd", dbg["dbg_outt"].ap(), rs_out, waits=[t_cc], sem="DBH")
    statbanks = {0: (pA, pB), 1: (pF, pH)}

    def ln_finish_var(sB, rb_slot):
        rbs = rb_sb[:, rb_slot * 512:(rb_slot + 1) * 512]
        t_ln = rd("scalar", sB, lambda e, s=sB, o=rbs:
                  e.activation(o, s.t[:], AF.Ln, bias=c_cv[:, 2:3]))
        return sch.op("scalar", lambda e, o=rbs: e.activation(o, o, AF.Exp,
                                                              scale=c_cv[:, 5:6]),
                      [t_ln], inc="SC"), rbs

    outt_guard = wo_free[0] + wo_free[1] + [t_cc] + \
        ([dbg_ag_tok] if DEBUG else [])
    t_outt = None
    for m in range(32):
        t_outt = sch.dma("gpsimd", APc(outt, m * 1024, [[32768, 128], [1, 1024]]),
                         rs_out[m], waits=outt_guard, sem="OT")
    y_tok = {}
    y3_tok = {}
    sqtok = {}
    fslot = [fstg_gate0, fstg_gate1]
    for tc in range(2):
        sA, sB = statbanks[tc]
        for m in range(32):
            osl = APc(outt, m * 1024 + tc * 512, [[32768, 128], [1, 512]])
            mm(sA, c_avg[:], osl, m == 0, m == 31, waits=[t_outt])
        for m in range(32):
            sl = m % 2
            osl = APc(outt, m * 1024 + tc * 512, [[32768, 128], [1, 512]])
            t_1 = sch.op("vector",
                         lambda e, o=osl, s=sA: e.tensor_sub(o, o, s.t[:]),
                         [sA.wtok], inc="V")
            sA.add_reader(t_1)
            sqs = bstg[:, sl * 512:(sl + 1) * 512]
            t_sq = sch.op("scalar",
                          lambda e, o=sqs, i=osl: e.activation(o, i, AF.Square),
                          [t_1, sq_free[sl]], inc="SC")
            sq_free[sl] = None
            tb = mm(sB, c_avg[:], sqs, m == 0, m == 31, waits=[t_sq], sig=True)
            sq_free[sl] = tb
            sqtok[(m, tc)] = t_sq
        t_rb, rbs = ln_finish_var(sB, tc)
        for m in range(32):
            sl = m % 2
            osl = APc(outt, m * 1024 + tc * 512, [[32768, 128], [1, 512]])
            xtile = fstg[:, sl * 512:(sl + 1) * 512]
            t_x = sch.dma("sync", xtile,
                          APc(xTo.tensor, m * 128 * TOWN + tc * 512, [[TOWN, 128], [1, 512]]),
                          waits=[fslot[sl]], sem=f"F{sl}")
            t_2 = sch.op("vector",
                         lambda e, o=osl, r=rbs: e.tensor_mul(o, o, r),
                         [t_rb, sqtok[(m, tc)]], inc="V")
            t_3 = sch.op("vector", lambda e, o=osl, x_=xtile: e.tensor_add(o, o, x_),
                         [t_2, t_x], inc="V")
            fslot[sl] = t_3
            y3_tok[(m, tc)] = t_3
            y_tok[(m, tc)] = sch.dma("gpsimd", ysp[m][:, tc * 512:(tc + 1) * 512],
                                     osl, waits=[t_3], sem=f"YW{2 + sl}")
    if DEBUG:
        sch.dma("gpsimd", dbg["dbg_y"].ap(), ysp, waits=[y_tok[(31, 1)], y_tok[(30, 1)]],
                sem="DBH")

    smalls = []
    for dst, src, cols in ((wqfb, WqfT, 256), (wkfb, WkfT, 256), (atb, ATt, 128)):
        for g8 in range(4):
            inap = APc(src.tensor, (g8 * 8 * 128) * cols,
                       [[cols, 128], [128 * cols, 8], [1, cols]])
            outap = APc(dst, g8 * 8 * cols, [[IC * cols, 128], [cols, 8], [1, cols]])
            smalls.append(sch.dma("gpsimd", outap, inap, waits=wo_free[0] + wo_free[1],
                                  sem="SM"))
    smalls.append(sch.dma("gpsimd", bmtb[:], BmTt, sem="SM"))
    t_smalls = smalls[-1]

    ln2_tok = {}
    for tc in range(2):
        sA, sB = statbanks[tc]
        for m in range(32):
            osl = APc(outt, m * 1024 + tc * 512, [[32768, 128], [1, 512]])
            mm(sA, c_avg[:], osl, m == 0, m == 31, waits=[y3_tok[(m, tc)]])
        for m in range(32):
            sl = m % 2
            osl = APc(outt, m * 1024 + tc * 512, [[32768, 128], [1, 512]])
            dst = APc(ln2, m * 1024 + tc * 512, [[32768, 128], [1, 512]])
            t_1 = sch.op("vector",
                         lambda e, o=dst, a=osl, s=sA: e.tensor_sub(o, a, s.t[:]),
                         [ln2_tok.get((m - 2, tc)), sA.wtok], inc="V")
            sA.add_reader(t_1)
            sqs = bstg[:, (2 + sl) * 512:(3 + sl) * 512]
            t_sq = sch.op("scalar",
                          lambda e, o=sqs, i=dst: e.activation(o, i, AF.Square),
                          [t_1, sq_free[2 + sl]], inc="SC")
            tb = mm(sB, c_avg[:], sqs, m == 0, m == 31, waits=[t_sq], sig=True)
            sq_free[2 + sl] = tb
            ln2_tok[(m, tc)] = t_1
        t_rb, rbs = ln_finish_var(sB, tc)
        l3 = APc(ln2, tc * 512, [[32768, 128], [1024, 32], [1, 512]])
        rbc2 = APc(rb_sb, tc * 512, [[1024, 128], [0, 32], [1, 512]])
        t_nm2 = sch.op("vector", lambda e, a=l3, r=rbc2: e.tensor_mul(a, a, r),
                       [t_rb, ln2_tok[(31, tc)], sq_free[2], sq_free[3]], inc="V")
        for m in range(32):
            ln2_tok[(m, tc)] = t_nm2
    if DEBUG:
        sch.dma("gpsimd", dbg["dbg_ln2"].ap(), ln2[:], waits=[ln2_tok[(31, 1)]],
                sem="DBH")

    def ln2ap(ic, tc):
        return APc(ln2, ic * 1024 + tc * 512, [[32768, 128], [1, 512]])

    ln2_all = ln2_tok[(31, 1)]
    qf_toks = []
    for nb in range(2):
        for tc in range(2):
            bank = pC if (nb * 2 + tc) % 2 == 0 else pD
            for ic in range(IC):
                mm(bank, APc(wqfb, ic * 256 + nb * 128, [[8192, 128], [1, 128]]),
                   ln2ap(ic, tc), ic == 0, ic == IC - 1,
                   waits=[t_smalls, ln2_tok[(ic, tc)]])
            qf_toks.append(rd("scalar", bank,
                              lambda e, o=qf[:, nb * 1024 + tc * 512: nb * 1024 + tc * 512 + 512],
                              b=bank: e.copy(o, b.t[:])))
            bank2 = pD if (nb * 2 + tc) % 2 == 0 else pC
            for ic in range(IC):
                mm(bank2, APc(wkfb, ic * 256 + nb * 128, [[8192, 128], [1, 128]]),
                   ln2ap(ic, tc), ic == 0, ic == IC - 1, waits=[t_smalls])
            qf_toks.append(rd("scalar", bank2,
                              lambda e, o=kf[:, nb * 1024 + tc * 512: nb * 1024 + tc * 512 + 512],
                              b=bank2: e.copy(o, b.t[:])))
    t_qk = sch.op("vector", lambda e: e.tensor_mul(qk[:], qf[:], kf[:]),
                  qf_toks, inc="V")
    wexp_tok = {}
    tiny_chain = None
    for tc in range(2):
        for nb in range(2):
            seg = c_seg0 if nb == 0 else c_seg1
            mm(pE, seg[:], APc(qk, nb * 1024 + tc * 512, [[2048, 128], [1, 512]]),
               nb == 0, nb == 1, waits=[t_qk], out=pE.t[0:8, 0:512])
        t_sce = rd("scalar", pE, lambda e: e.copy(sce[:], pE.t[0:8, 0:512]))
        tr_toks = []
        for ts in range(4):
            w_ = list(pG.rtoks()) + [t_sce]
            tok = sch.op("tensor",
                         lambda e, i=sce[:, ts * 128:(ts + 1) * 128]:
                         e.transpose(pG.t[0:128, 0:8], i, c_idf[0:8, 0:8]),
                         w_, inc="T")
            pG.readers = {}
            pG.wtok = tok
            tr_toks.append(rd("scalar", pG,
                              lambda e, o=snat[:, ts * 8:(ts + 1) * 8]:
                              e.copy(o, pG.t[0:128, 0:8])))
        wn_toks = []
        for ts in range(4):
            t_ex = sch.op("scalar",
                          lambda e, i=snat[:, ts * 8:(ts + 1) * 8],
                          o=wn[:, ts * 8:(ts + 1) * 8]:
                          e.activation(o, i, AF.Exp, scale=c_cv[:, 3:4]),
                          [tr_toks[ts], tiny_chain], inc="SC")
            t_sm = sch.op("vector",
                          lambda e, i=wn[:, ts * 8:(ts + 1) * 8]:
                          e.reduce_sum(tiny[:, 4:5], i, AX.X), [t_ex], inc="V")
            t_l1 = sch.op("scalar",
                          lambda e: e.activation(tiny[:, 5:6], tiny[:, 4:5], AF.Ln),
                          [t_sm], inc="SC")
            t_rc = sch.op("scalar",
                          lambda e: e.activation(tiny[:, 5:6], tiny[:, 5:6], AF.Exp,
                                                 scale=c_cv[:, 6:7]),
                          [t_l1], inc="SC")
            t_n = sch.op("vector",
                         lambda e, o=wn[:, ts * 8:(ts + 1) * 8]:
                         e.tensor_scalar_mul(o, o, tiny[:, 5:6]), [t_rc], inc="V")
            tiny_chain = t_n
            wn_toks.append(t_n)
        for ts in range(4):
            w_ = list(pG.rtoks()) + [wn_toks[ts]]
            tok = sch.op("tensor",
                         lambda e, i=wn[:, ts * 8:(ts + 1) * 8]:
                         e.transpose(pG.t[0:8, 0:128], i, c_idf[:]),
                         w_, inc="T")
            pG.readers = {}
            pG.wtok = tok
            rd("scalar", pG,
               lambda e, o=w8[:, ts * 128:(ts + 1) * 128]: e.copy(o, pG.t[0:8, 0:128]))
        mm(pG, c_e8[:], w8[:], True, True)
        wexp_tok[tc] = rd("scalar", pG,
                          lambda e, o=wexp[:, tc * 512:(tc + 1) * 512]: e.copy(o, pG.t[:]))
    if DEBUG:
        sch.dma("gpsimd", dbg["dbg_qf"].ap(), qf[:], waits=qf_toks, sem="DBH")
        sch.dma("gpsimd", dbg["dbg_kf"].ap(), kf[:], waits=qf_toks, sem="DBH")
        sch.dma("gpsimd", dbg["dbg_sn"].ap()[:, 0:32], snat[:], waits=[tiny_chain],
                sem="DBH")
        sch.dma("gpsimd", dbg["dbg_sn"].ap()[:, 32:64], wn[:], waits=[tiny_chain],
                sem="DBH")
        sch.dma("gpsimd", dbg["dbg_w8"].ap(), w8[:], waits=[wexp_tok[1]], sem="DBH")
        sch.dma("gpsimd", dbg["dbg_wexp"].ap(), wexp[:], waits=[wexp_tok[1]],
                sem="DBH")
    g_tok = {}
    for tc in range(2):
        bank = pC if tc == 0 else pD
        for ic in range(IC):
            mm(bank, APc(atb, ic * 128, [[4096, 128], [1, 128]]), ln2ap(ic, tc),
               ic == 0, ic == IC - 1, waits=[t_smalls])
        g_tok[tc] = rd("vector", bank,
                       lambda e, o=gsb[:, tc * 512:(tc + 1) * 512], b=bank,
                       wv_=wexp[:, tc * 512:(tc + 1) * 512]:
                       e.tensor_mul(o, b.t[:], wv_), waits=[wexp_tok[tc]])

    if DEBUG:
        sch.dma("gpsimd", dbg["dbg_g"].ap(), gsb[:], waits=[g_tok[0], g_tok[1]],
                sem="DBH")

    z_tok = {}
    wbufs = [wblk, wblk2]
    wfree = {0: [g_tok[0], g_tok[1], t_qk],
             1: [g_tok[0], g_tok[1], t_qk] + ([dbg_ag_tok] if DEBUG else [])}
    zslot = [sq_free[0], sq_free[1]]
    for blk_i in range(8):
        wb = wbufs[blk_i % 2]
        wd = []
        for g8 in range(4):
            inap = APc(WT.tensor, (g8 * 8 * 128) * D + blk_i * 512,
                       [[D, 128], [128 * D, 8], [1, 512]])
            outap = APc(wb, g8 * 8 * 512, [[16384, 128], [512, 8], [1, 512]])
            wd.append(sch.dma("gpsimd", outap, inap, waits=wfree[blk_i % 2],
                              sem=f"WB{(blk_i % 2) * 4 + g8}"))
        last_g = None
        for mo in range(4):
            m = blk_i * 4 + mo
            for tc in range(2):
                bank = pC if (mo * 2 + tc) % 2 == 0 else pD
                for ic in range(IC):
                    mm(bank, APc(wb, ic * 512 + mo * 128, [[16384, 128], [1, 128]]),
                       ln2ap(ic, tc), ic == 0, False, waits=[wd[ic // 8], ln2_all])
                last_g = mm(bank,
                            bmtb[:, blk_i * 512 + mo * 128: blk_i * 512 + mo * 128 + 128],
                            APc(gsb, tc * 512, [[1024, 128], [1, 512]]),
                            False, True, waits=[g_tok[tc], t_smalls])
                sl = (m * 2 + tc) % 2
                zst = bstg[:, sl * 512:(sl + 1) * 512]
                t_ev = rd("scalar", bank,
                          lambda e, o=zst, b=bank, mi=m:
                          e.activation(o, b.t[:], AF.Identity, bias=c_b[:, mi:mi + 1]),
                          waits=[zslot[sl], z_tok.get((m - 1, tc))])
                sA, sB = statbanks[tc]
                tb = mm(sA, c_avg[:], zst, m == 0, m == 31, waits=[t_ev], sig=True)
                zslot[sl] = tb
                z_tok[(m, tc)] = sch.dma("gpsimd", zsp[m][:, tc * 512:(tc + 1) * 512],
                                         zst, waits=[t_ev, tb], sem=f"ZW{tc}")
        wfree[blk_i % 2] = [last_g]
    if DEBUG:
        sch.dma("gpsimd", dbg["dbg_z"].ap(), zsp,
                waits=[z_tok[(31, 0)], z_tok[(31, 1)]], sem="DBH")

    oslot = [None] * 2
    zslot = [z_tok[(31, 0)], z_tok[(31, 1)]]
    ln2_dead = [wfree[0], wfree[1]]
    sqtok3 = {}
    for tc in range(2):
        sA, sB = statbanks[tc]
        for m in range(32):
            sl = m % 2
            ztile = bstg[:, sl * 512:(sl + 1) * 512]
            t_z = sch.dma("gpsimd", ztile, zsp[m][:, tc * 512:(tc + 1) * 512],
                          waits=[z_tok[(m, tc)], zslot[sl]], sem=f"B{sl}")
            dsl = APc(ln2, m * 1024 + tc * 512, [[32768, 128], [1, 512]])
            t_1 = sch.op("vector",
                         lambda e, o=dsl, a=ztile, s=sA: e.tensor_sub(o, a, s.t[:]),
                         [t_z, sA.wtok] + ln2_dead, inc="V")
            sA.add_reader(t_1)
            sqs = bstg[:, sl * 512:(sl + 1) * 512]
            t_sq = sch.op("scalar", lambda e, o=sqs, i=dsl: e.activation(o, i, AF.Square),
                          [t_1], inc="SC")
            tb = mm(sB, c_avg[:], sqs, m == 0, m == 31, waits=[t_sq], sig=True)
            zslot[sl] = tb
            sqtok3[(m, tc)] = t_sq
        t_rb, rbs = ln_finish_var(sB, tc)
        for m in range(32):
            sl = m % 2
            dsl = APc(ln2, m * 1024 + tc * 512, [[32768, 128], [1, 512]])
            ytile = bstg[:, (2 + sl) * 512:(3 + sl) * 512]
            t_y = sch.dma("gpsimd", ytile, ysp[m][:, tc * 512:(tc + 1) * 512],
                          waits=[sq_free[2 + sl], y_tok[(30, 1)], y_tok[(31, 1)]],
                          sem=f"B{2 + sl}")
            sq_free[2 + sl] = None
            otile = fstg[:, sl * 512:(sl + 1) * 512]
            t_2 = sch.op("vector",
                         lambda e, o=otile, a=dsl, r=rbs: e.tensor_mul(o, a, r),
                         [t_rb, sqtok3[(m, tc)], oslot[sl]], inc="V")
            t_3 = sch.op("vector", lambda e, o=otile, y_=ytile: e.tensor_add(o, o, y_),
                         [t_2, t_y], inc="V")
            sq_free[2 + sl] = t_3
            oslot[sl] = sch.dma("sync",
                                APc(out_t, m * 128 * TOWN + tc * 512,
                                    [[TOWN, 128], [1, 512]]),
                                otile, waits=[t_3], sem=f"F{sl}")

    finals = [(sm, sch.cnt[sm]) for sm in sorted(sch.dma_sems)] +              [("CC", sch.cnt["CC"])]
    sch.op("sync", lambda e: None, finals)
    sch.op("gpsimd", lambda e: None, finals)

    sems = {n: st.enter_context(nc.semaphore(n))
            for n in sorted(set(sch.cnt) | {"T", "V", "SC"})}
    with nc.Block() as blkk:
        def make_body(eng):
            def body(e):
                for wl, fn, inc, n in sch.q[eng]:
                    for s_, v_ in wl:
                        e.wait_ge(sems[s_], v_)
                    inst = fn(e)
                    if inc is not None:
                        inst.then_inc(sems[inc], n)
            return body
        blkk.sync(make_body("sync"))
        blkk.gpsimd(make_body("gpsimd"))
        blkk.tensor(make_body("tensor"))
        blkk.vector(make_body("vector"))
        blkk.scalar(make_body("scalar"))

    st.close()
    return nc


def _consts():
    bf = ml_dtypes.bfloat16
    avg = np.full((128, 128), 1.0 / 4096.0, np.float32).astype(bf)
    ones = np.ones((128, 128), np.float32).astype(bf)
    ident = np.eye(128, dtype=np.float32)
    perm = np.zeros((128, 128), np.float32)
    for i in range(128):
        perm[i ^ 1, i] = 1.0
    rk, rq = np.meshgrid(np.arange(128), np.arange(128), indexing="ij")
    maskT = np.where(rk <= rq, 0.0, -1e9).astype(np.float32)
    maskq = maskT.T.copy()
    seg0 = np.zeros((128, 8), np.float32)
    seg1 = np.zeros((128, 8), np.float32)
    for p in range(128):
        seg0[p, p // 32] = 1.0
        seg1[p, 4 + p // 32] = 1.0
    e8 = np.zeros((8, 128), np.float32)
    for e_ in range(8):
        e8[e_, e_ * 16:(e_ + 1) * 16] = 1.0
    cv = np.zeros((128, 12), np.float32)
    cv[:, 0] = SM_SHIFT
    cv[:, 1] = SM_SCALE
    cv[:, 2] = EPS
    cv[:, 3] = E_SCALE
    cv[:, 4] = -10.0
    cv[:, 5] = -0.5
    cv[:, 6] = -1.0
    cv[:, 7] = 2.0 ** -6
    cv[:, 8] = -6.0 * np.log(2.0)
    cv[:, 9] = -SM_SCALE
    sel = np.zeros((32, 4096), np.float32)
    for g in range(32):
        sel[g, g * 128:(g + 1) * 128] = 1.0
    pos = np.arange(S, dtype=np.float64)
    invf = 1.0 / (10000.0 ** (np.arange(0, HD, 2, dtype=np.float64) / HD))
    f = pos[None, :] * invf[(np.arange(128) // 2), None]
    cos = np.cos(f).astype(np.float32)
    sgn = np.where(np.arange(128) % 2 == 0, -1.0, 1.0)
    sin = (np.sin(f) * sgn[:, None]).astype(np.float32)
    return dict(avgb=avg, onesb=ones, identf=ident, epermb=perm.astype(bf),
                maskTb=maskT.astype(bf), maskqb=maskq.astype(bf),
                seg0=seg0.astype(bf), seg1=seg1.astype(bf),
                exp8f=e8.astype(bf), constv=cv, selb=sel.astype(bf), cosf=cos, sinf=sin)


def kernel(L0_attn_hidden_states, L0_attn_q_codes, L0_attn_q_scales,
           L0_attn_k_codes, L0_attn_k_scales, L0_attn_v_codes, L0_attn_v_scales,
           L0_attn_o_codes, L0_attn_o_scales, L0_ffn_W, L0_ffn_b,
           L0_ffn_Wq, L0_ffn_Wk, L0_ffn_A, L0_ffn_Bm):
    x = np.asarray(L0_attn_hidden_states, np.float32)
    cons = _consts()
    WTh = np.ascontiguousarray(np.asarray(L0_ffn_W, np.float32).T)
    WqfTh = np.ascontiguousarray(np.asarray(L0_ffn_Wq, np.float32).T)
    WkfTh = np.ascontiguousarray(np.asarray(L0_ffn_Wk, np.float32).T)
    ATh = np.ascontiguousarray(
        np.asarray(L0_ffn_A, np.float32).transpose(2, 0, 1).reshape(D, 128))
    BmTh = np.ascontiguousarray(
        np.asarray(L0_ffn_Bm, np.float32).transpose(0, 2, 1).reshape(128, D))
    bhh = np.ascontiguousarray(np.asarray(L0_ffn_b, np.float32).reshape(32, 128).T)
    ocodes_full = np.asarray(L0_attn_o_codes, np.int32)
    osc_full = np.asarray(L0_attn_o_scales, np.float32)
    ocodh = {}
    osch = {}
    for hh_ in range(2):
        sl_ = ocodes_full[hh_ * 2048:(hh_ + 1) * 2048]
        ocodh[hh_] = np.ascontiguousarray(
            sl_.reshape(2048, 16, 256).transpose(1, 0, 2))
        osch[hh_] = np.ascontiguousarray(osc_full[hh_ * 16:(hh_ + 1) * 16])
    qcodes = np.asarray(L0_attn_q_codes, np.int32)
    kcodes = np.asarray(L0_attn_k_codes, np.int32)
    vcodes = np.asarray(L0_attn_v_codes, np.int32)

    in_maps = []
    for c in range(8):
        b, hh = c // 2, c % 2
        qslice = qcodes[:, hh * 2048:(hh + 1) * 2048]
        m = dict(cons)
        m.update(
            xT=np.ascontiguousarray(x[b].T),
            xTo=np.ascontiguousarray(x[b, hh * TOWN:(hh + 1) * TOWN].T),
            qcod=np.ascontiguousarray(qslice.reshape(D, 4, 512).transpose(1, 0, 2)),
            kcod=np.ascontiguousarray(kcodes[:, hh * 512:(hh + 1) * 512]),
            vcod=np.ascontiguousarray(vcodes[:, hh * 512:(hh + 1) * 512]),
            ocod=ocodh[hh],
            qsc=np.ascontiguousarray(
                np.asarray(L0_attn_q_scales, np.float32)[:, hh * 2048:(hh + 1) * 2048]),
            ksc=np.ascontiguousarray(
                np.asarray(L0_attn_k_scales, np.float32)[:, hh * 512:(hh + 1) * 512]),
            vsc=np.ascontiguousarray(
                np.asarray(L0_attn_v_scales, np.float32)[:, hh * 512:(hh + 1) * 512]),
            osc=osch[hh], WT=WTh, WqfT=WqfTh, WkfT=WkfTh, ATt=ATh, BmTt=BmTh, bh=bhh,
        )
        in_maps.append(m)

    nc = build_nc()
    res = run_bass_kernel_spmd(nc, in_maps, core_ids=list(range(8)))

    outp = np.empty((4, S, D), np.float32)
    for c in range(8):
        b, hh = c // 2, c % 2
        outp[b, hh * TOWN:(hh + 1) * TOWN, :] = \
            np.asarray(res.results[c]["outT"], np.float32).T
    if DEBUG:
        kernel._dbg = [res.results[c] for c in range(8)]
    return outp


# version 07
# speedup vs baseline: 1.0470x; 1.0470x over previous
import numpy as np
import ml_dtypes
from contextlib import ExitStack

import concourse.bass as bass
import concourse.mybir as mybir
from concourse.bass_utils import run_bass_kernel_spmd

F32 = mybir.dt.float32
BF16 = mybir.dt.bfloat16
I32 = mybir.dt.int32
AF = mybir.ActivationFunctionType
ALU = mybir.AluOpType
AX = mybir.AxisListType

D = 4096
S = 2048
IC = 32
NHL = 16
NKVL = 4
HD = 128
TOWN = 1024
SM_SCALE = float(HD) ** -0.5
SM_SHIFT = -40.0
EPS = 1e-5
E_SCALE = 1.0 / np.sqrt(32.0)

DEBUG = False

ENGS = ("sync", "gpsimd", "tensor", "vector", "scalar")
SEM_ENG = {"T": "tensor", "V": "vector", "SC": "scalar"}


class Sched:
    def __init__(self):
        self.q = {e: [] for e in ENGS}
        self.lines = {e: [] for e in ENGS}
        self.cnt = {}
        self.wm = {e: {} for e in ENGS}
        self.dma_sems = set()

    def op(self, eng, fn, waits=(), inc=None, n=1):
        import sys
        fr = sys._getframe(1)
        while fr.f_code.co_name in ("dma", "mm", "rd", "ln_finish_var"):
            fr = fr.f_back
        self.lines[eng].append(fr.f_lineno)
        flat = []

        def _fl(t):
            if t is None:
                return
            if isinstance(t, tuple) and t and isinstance(t[0], str):
                flat.append(t)
            elif isinstance(t, (list, tuple)):
                for u in t:
                    _fl(u)
            else:
                flat.append(t)
        for t in waits:
            _fl(t)
        wl = {}
        for t in flat:
            if t is None:
                continue
            s, v = t
            if SEM_ENG.get(s) == eng:
                continue
            if v > wl.get(s, -1):
                wl[s] = v
        wl2 = []
        for s, v in wl.items():
            if v <= self.wm[eng].get(s, 0):
                continue
            self.wm[eng][s] = v
            wl2.append((s, v))
        tok = None
        if inc is not None:
            self.cnt[inc] = self.cnt.get(inc, 0) + n
            tok = (inc, self.cnt[inc])
        self.q[eng].append((wl2, fn, inc, n))
        return tok

    def dma(self, eng, out_ap, in_ap, waits=(), sem=None):
        assert sem is not None, "every DMA needs a dedicated slot semaphore"
        self.dma_sems.add(sem)
        return self.op(eng, lambda e, o=out_ap, i=in_ap: e.dma_start(o, i),
                       waits, inc=sem, n=16)


class Bank:

    def __init__(self, t):
        self.t = t
        self.readers = {}
        self.wtok = None

    def rtoks(self):
        return list(self.readers.items())

    def add_reader(self, tok):
        if tok[1] > self.readers.get(tok[0], -1):
            self.readers[tok[0]] = tok[1]


def build_nc():
    nc = bass.Bass("TRN2", target_bir_lowering=False)

    def din(name, shape, dt):
        return nc.dram_tensor(name, list(shape), dt, kind="ExternalInput").ap()

    xT = din("xT", (D, S), F32)
    xTo = din("xTo", (D, TOWN), F32)
    qcod = din("qcod", (4, D, 512), I32)
    kcod = din("kcod", (D, 512), I32)
    vcod = din("vcod", (D, 512), I32)
    ocod = din("ocod", (16, 2048, 256), I32)
    qsc = din("qsc", (32, 2048), F32)
    ksc = din("ksc", (32, 512), F32)
    vsc = din("vsc", (32, 512), F32)
    osc = din("osc", (16, D), F32)
    WT = din("WT", (D, D), F32)
    WqfT = din("WqfT", (D, 256), F32)
    WkfT = din("WkfT", (D, 256), F32)
    ATt = din("ATt", (D, 128), F32)
    BmTt = din("BmTt", (128, D), F32)
    bh = din("bh", (128, 32), F32)
    avgb = din("avgb", (128, 128), BF16)
    onesb = din("onesb", (128, 128), BF16)
    identf = din("identf", (128, 128), F32)
    epermb = din("epermb", (128, 128), BF16)
    maskTb = din("maskTb", (128, 128), BF16)
    maskqb = din("maskqb", (128, 128), BF16)
    seg0 = din("seg0", (128, 8), BF16)
    seg1 = din("seg1", (128, 8), BF16)
    exp8f = din("exp8f", (8, 128), BF16)
    constv = din("constv", (128, 12), F32)
    selb = din("selb", (32, D), BF16)
    cosf = din("cosf", (128, S), F32)
    sinf = din("sinf", (128, S), F32)

    out_t = nc.dram_tensor("outT", [D, TOWN], F32, kind="ExternalOutput")

    ln1sp_t = nc.dram_tensor("ln1sp", [4, 128, IC * 512], BF16)
    ln1sp = ln1sp_t.ap()
    rs_in = nc.dram_tensor("rs_in", [2, 32, 128, TOWN], BF16).ap()
    rs_out = nc.dram_tensor("rs_out", [32, 128, TOWN], BF16).ap()
    ysp = nc.dram_tensor("ysp", [32, 128, TOWN], BF16).ap()
    zsp = nc.dram_tensor("zsp", [32, 128, TOWN], BF16).ap()
    outtsp = nc.dram_tensor("outtsp", [32, 128, TOWN], BF16).ap()

    dbg = {}
    if DEBUG:
        for name, shape, dt in [
            ("dbg_ln1", (4, 128, IC * 512), BF16), ("dbg_kt", (128, 8192), BF16),
            ("dbg_v", (128, 8192), BF16), ("dbg_qt", (128, NHL * S), BF16),
            ("dbg_ag", (128, NHL * S), BF16),
            ("dbg_outt", (32, 128, TOWN), BF16), ("dbg_y", (32, 128, TOWN), BF16),
            ("dbg_ln2", (128, IC * TOWN), BF16), ("dbg_z", (32, 128, TOWN), BF16),
            ("dbg_wk", (128, 16384), BF16), ("dbg_wv", (128, 16384), BF16),
            ("dbg_qf", (128, 2048), BF16), ("dbg_kf", (128, 2048), BF16),
            ("dbg_swp", (128, 1024), BF16), ("dbg_krb", (128, 2048), BF16),
            ("dbg_sn", (128, 64), F32), ("dbg_w8", (8, 512), F32),
            ("dbg_sp", (128, 2048), BF16),
            ("dbg_wexp", (128, 1024), BF16), ("dbg_g", (128, 1024), BF16),
        ]:
            dbg[name] = nc.dram_tensor(name, list(shape), dt, kind="ExternalOutput")

    st = ExitStack()
    sb = lambda name, shp, dt: st.enter_context(nc.sbuf_tensor(name, shp, dt))
    psum = lambda name: st.enter_context(nc.psum_tensor(name, [128, 512], F32))

    reg1 = sb("reg1", [128, 65536], mybir.dt.uint8)
    reg2 = sb("reg2", [128, 65536], mybir.dt.uint8)
    reg3 = sb("reg3", [128, 49152], mybir.dt.uint8)

    def view(region, off_bytes, name, shape, dt):
        base = nc.lookup_mloc(region).addr
        return nc.alloc_sbuf_tensor_at(name, list(shape), dt, offset=base + off_bytes)

    wk = view(reg1, 0, "wk", [128, 16384], BF16)
    wv = view(reg1, 32768, "wv", [128, 16384], BF16)
    wq = view(reg1, 0, "wq", [128, 16384], BF16)
    qta = view(reg1, 32768, "qta", [128, 16384], BF16)
    otl = view(reg1, 0, "otl", [128, 32768], BF16)
    qf = view(reg1, 0, "qf", [128, 2048], BF16)
    kf = view(reg1, 4096, "kf", [128, 2048], BF16)
    qk = view(reg1, 8192, "qk", [128, 2048], BF16)
    w8 = view(reg1, 12288, "w8", [8, 512], BF16)
    sce = view(reg1, 14336, "sce", [8, 512], F32)
    snat = view(reg1, 16384, "snat", [128, 32], F32)
    wn = view(reg1, 16512, "wn", [128, 32], F32)
    wexp = view(reg1, 16640, "wexp", [128, 1024], BF16)
    gsb = view(reg1, 18688, "gsb", [128, 1024], BF16)
    outt = view(reg1, 0, "outt", [128, 32768], BF16)
    cosfs = view(reg1, 0, "cosfs", [128, S], F32)
    sinfs = view(reg1, 8192, "sinfs", [128, S], F32)
    xbf = view(reg2, 0, "xbf", [128, 32768], BF16)
    rr = view(reg2, 0, "rr", [128, 16384], BF16)
    qtb = view(reg2, 32768, "qtb", [128, 16384], BF16)
    pt = view(reg2, 0, "pt", [128, 16384], BF16)
    ocod_sb = view(reg2, 0, "ocod_sb", [128, 8192], I32)
    ln2 = view(reg2, 0, "ln2", [128, 32768], BF16)
    kt = view(reg3, 0, "kt", [128, 8192], BF16)
    vsb = view(reg3, 16384, "vsb", [128, 8192], BF16)
    cosb = view(reg3, 32768, "cosb", [128, S], BF16)
    sinb = view(reg3, 36864, "sinb", [128, S], BF16)
    selv = view(reg3, 40960, "selv", [32, 4096], BF16)
    wo = view(reg3, 0, "wo", [128, 8192], BF16)
    wqfb = view(reg3, 0, "wqfb", [128, 8192], BF16)
    wkfb = view(reg3, 16384, "wkfb", [128, 8192], BF16)
    atb = view(reg3, 32768, "atb", [128, 4096], BF16)
    bmtb = view(reg3, 40960, "bmtb", [128, 4096], BF16)
    wblk = view(reg3, 0, "wblk", [128, 16384], BF16)
    wblk2 = view(reg1, 24576, "wblk2", [128, 16384], BF16)

    c_avg = sb("c_avg", [128, 128], BF16)
    c_ones = sb("c_ones", [128, 128], BF16)
    c_idf = sb("c_idf", [128, 128], F32)
    c_perm = sb("c_perm", [128, 128], BF16)
    c_mT = sb("c_mT", [128, 128], BF16)
    c_mq = sb("c_mq", [128, 128], BF16)
    c_seg0 = sb("c_seg0", [128, 8], BF16)
    c_seg1 = sb("c_seg1", [128, 8], BF16)
    c_e8 = sb("c_e8", [8, 128], BF16)
    c_cv = sb("c_cv", [128, 12], F32)
    c_b = sb("c_b", [128, 32], F32)
    ctile = sb("ctile", [128, 1024], I32)
    scsl = sb("scsl", [32, 512], F32)
    scbf = sb("scbf", [32, 512], BF16)
    m_sb = sb("m_sb", [128, 512], F32)
    var_sb = sb("var_sb", [128, 512], F32)
    rb_sb = sb("rb_sb", [128, 1024], F32)
    bstg = sb("bstg", [128, 2048], BF16)
    fstg = sb("fstg", [128, 1024], F32)
    pq = sb("pq", [128, 128], F32)
    trsc = sb("trsc", [128, 512], F32)
    tiny = sb("tiny", [128, 8], F32)

    pA, pB, pC, pD, pE, pF, pG, pH = (Bank(psum(n)) for n in
                                      ("pA", "pB", "pC", "pD", "pE", "pF", "pG", "pH"))

    sch = Sched()
    APc = bass.AP
    lnf = {"chain": None}

    def mm(bank, lhsT, rhs, start, stop, waits=(), out=None, sig=False):
        o = out if out is not None else bank.t[:]
        w = list(waits)
        if start:
            w += bank.rtoks()
        tok = sch.op("tensor",
                     lambda e, o=o, l=lhsT, r=rhs, s1=start, s2=stop:
                     e.matmul(o, l, r, start=s1, stop=s2),
                     w, inc=("T" if (stop or sig) else None))
        if start:
            bank.readers = {}
        if stop:
            bank.wtok = tok
        return tok

    def rd(eng, bank, fn, waits=()):
        sem = {"vector": "V", "scalar": "SC"}[eng]
        tok = sch.op(eng, fn, list(waits) + [bank.wtok], inc=sem)
        bank.add_reader(tok)
        return tok

    for dst, src in [(c_avg, avgb), (c_ones, onesb), (c_idf, identf),
                     (c_perm, epermb), (c_mT, maskTb), (c_mq, maskqb),
                     (c_seg0, seg0), (c_seg1, seg1), (c_e8, exp8f),
                     (c_cv, constv), (c_b, bh)]:
        sch.dma("sync", dst[:], src, sem="C")
    sch.dma("sync", cosfs[:], cosf, sem="C")
    sch.dma("sync", sinfs[:], sinf, sem="C")
    tok_c = sch.dma("sync", selv[:], selb, sem="C")
    t_cosb = sch.op("scalar", lambda e: e.copy(cosb[:], cosfs[:]), [tok_c], inc="SC")
    t_sinb = sch.op("scalar", lambda e: e.copy(sinb[:], sinfs[:]), [tok_c], inc="SC")

    sc_state = {"dma": None, "copy": None}

    def load_scales(src_ap, width, rows=32):
        t_sc = sch.dma("sync", scsl[0:rows, 0:width], src_ap,
                       waits=[sc_state["copy"]], sem="SCL")
        t_cp = sch.op("scalar",
                      lambda e, w=width, r=rows: e.copy(scbf[0:r, 0:w], scsl[0:r, 0:w]),
                      [t_sc] + pE.rtoks() + ([pE.wtok] if pE.wtok else []), inc="SC")
        sc_state["dma"] = t_sc
        sc_state["copy"] = t_cp
        return t_cp

    def bcast(ic, sc_tok, width, rows=32):
        return mm(pE, selv[0:rows, ic * 128:(ic + 1) * 128], scbf[0:rows, 0:width],
                  True, True, waits=[sc_tok, tok_c], out=pE.t[:, 0:width])

    def dequant(wdst_ap, cod_tok, ctile_ap, width, extra=()):
        tok = sch.op("vector",
                     lambda e, o=wdst_ap, c=ctile_ap, p=pE.t[:, 0:width]:
                     e.scalar_tensor_tensor(o, c, -8.0, p, ALU.add, ALU.mult),
                     list(extra) + [cod_tok, pE.wtok], inc="V")
        pE.add_reader(tok)
        return tok

    ct_free = [None, None]
    kv_deq_tok = None
    first_deq_guard = [t_cosb, t_sinb]
    for (wdst, cod, scv) in ((wk, kcod, ksc), (wv, vcod, vsc)):
        t_scb = load_scales(scv, 512)
        for ic in range(IC):
            slot = ic % 2
            ct = ctile[:, slot * 512:(slot + 1) * 512]
            t_cod = sch.dma("sync", ct, cod[ic * 128:(ic + 1) * 128, :],
                            waits=[ct_free[slot]], sem=f"CT{slot}")
            bcast(ic, t_scb, 512)
            kv_deq_tok = dequant(wdst[:, ic * 512:(ic + 1) * 512], t_cod, ct, 512,
                                 extra=first_deq_guard)
            first_deq_guard = []
            ct_free[slot] = kv_deq_tok

    dbg_guard = []
    if DEBUG:
        sch.dma("gpsimd", dbg["dbg_wk"].ap(), wk[:], waits=[kv_deq_tok], sem="DBG")
        t2_ = sch.dma("gpsimd", dbg["dbg_wv"].ap(), wv[:], waits=[kv_deq_tok],
                      sem="DBG")
        dbg_guard = [t2_]
    xslot_free = [[], []]
    sp_toks = [None] * 4
    ln1_tok = [None] * 4
    ktok = {}
    vtok = {}
    sq_free = [None] * 4
    rope_scr = [None, None]
    for ch in range(4):
        slot = ch % 2
        xoff = slot * 16384
        dx = []
        for g8 in range(4):
            inap = APc(xT.tensor, (g8 * 8 * 128) * S + ch * 512,
                       [[S, 128], [128 * S, 8], [1, 512]])
            outap = APc(xbf, xoff + g8 * 8 * 512, [[32768, 128], [512, 8], [1, 512]])
            dx.append(sch.dma("gpsimd", outap, inap, waits=xslot_free[slot],
                              sem=f"X{slot}{g8}"))
        for ic in range(IC):
            bslot = ic % 4
            xs = APc(xbf, xoff + ic * 512, [[32768, 128], [1, 512]])
            sqout = bstg[:, bslot * 512:(bslot + 1) * 512]
            tsq = sch.op("scalar",
                         lambda e, o=sqout, i=xs: e.activation(o, i, AF.Square),
                         [dx[ic // 8], sq_free[bslot]], inc="SC")
            mm(pA, c_avg[:], xs, ic == 0, ic == IC - 1, waits=[dx[ic // 8]])
            tb = mm(pB, c_avg[:], sqout, ic == 0, ic == IC - 1, waits=[tsq], sig=True)
            sq_free[bslot] = tb
        t_m = rd("scalar", pA, lambda e: e.copy(m_sb[:], pA.t[:]), waits=[lnf["chain"]])
        t_ms = sch.op("vector", lambda e: e.tensor_mul(var_sb[:], m_sb[:], m_sb[:]),
                      [t_m], inc="V")
        lnf["chain"] = t_ms
        t_var = rd("vector", pB, lambda e: e.tensor_sub(var_sb[:], pB.t[:], var_sb[:]),
                   [t_ms])
        rbs = rb_sb[:, slot * 512:slot * 512 + 512]
        t_ln = sch.op("scalar", lambda e, o=rbs: e.activation(o, var_sb[:], AF.Ln,
                                                              bias=c_cv[:, 2:3]),
                      [t_var], inc="SC")
        t_rb = sch.op("scalar", lambda e, o=rbs: e.activation(o, o, AF.Exp,
                                                              scale=c_cv[:, 5:6]),
                      [t_ln], inc="SC")
        x3 = APc(xbf, xoff, [[32768, 128], [512, 32], [1, 512]])
        mbc = APc(pA.t, 0, [[512, 128], [0, 32], [1, 512]])
        t_ns = rd("vector", pA, lambda e, a=x3, b=mbc: e.tensor_sub(a, a, b))
        rbc = APc(rb_sb, slot * 512, [[1024, 128], [0, 32], [1, 512]])
        t_nm = sch.op("vector", lambda e, a=x3, b=rbc: e.tensor_mul(a, a, b),
                      [t_rb, t_ns], inc="V")
        ln1_tok[ch] = t_nm
        t_sp = sch.dma("gpsimd", ln1sp[ch], APc(xbf, xoff, [[32768, 128], [1, 16384]]),
                       waits=[t_nm], sem=f"LW{slot}")
        last_kv = None
        for kvh in range(NKVL):
            bank = pC if kvh % 2 == 0 else pD
            for ic in range(IC):
                last_kv = mm(bank,
                             wk[:, ic * 512 + kvh * 128: ic * 512 + kvh * 128 + 128],
                             APc(xbf, xoff + ic * 512, [[32768, 128], [1, 512]]),
                             ic == 0, ic == IC - 1, waits=[t_nm, kv_deq_tok])
            rs = kvh % 2
            krb = bstg[:, rs * 512:(rs + 1) * 512]
            t_ev = rd("scalar", bank, lambda e, o=krb, b=bank: e.copy(o, b.t[:]),
                      waits=[rope_scr[rs], sq_free[rs]])
            t_pm = mm(pF, c_perm[:], krb, True, True, waits=[t_ev])
            t1s = bstg[:, (2 + rs) * 512:(3 + rs) * 512]
            cs = cosb[:, ch * 512:(ch + 1) * 512]
            sn = sinb[:, ch * 512:(ch + 1) * 512]
            t_t1 = sch.op("vector", lambda e, o=t1s, a=krb, c=cs: e.tensor_mul(o, a, c),
                          [t_ev, t_cosb, sq_free[2 + rs]], inc="V")
            ktd = kt[:, kvh * 2048 + ch * 512: kvh * 2048 + ch * 512 + 512]
            t_t2 = rd("vector", pF, lambda e, o=ktd, b=sn: e.tensor_mul(o, pF.t[:], b),
                      waits=[t_sinb])
            t_t3 = sch.op("vector", lambda e, o=ktd, a=t1s: e.tensor_add(o, o, a),
                          [t_t1, t_t2], inc="V")
            rope_scr[rs] = t_t3
            sq_free[rs] = [t_t1, t_pm]
            sq_free[2 + rs] = t_t3
            ktok[(kvh, ch)] = t_t3
        for ts in range(4):
            bank = pC if ts % 2 == 0 else pD
            for ic in range(IC):
                last_kv = mm(bank,
                             APc(xbf, xoff + ic * 512 + ts * 128, [[32768, 128], [1, 128]]),
                             wv[:, ic * 512:(ic + 1) * 512],
                             ic == 0, ic == IC - 1, waits=[t_nm, kv_deq_tok])
            kc = ch * 4 + ts
            t_ev = rd("scalar", bank,
                      lambda e, o=vsb[:, kc * 512:(kc + 1) * 512], b=bank: e.copy(o, b.t[:]))
            vtok[kc] = t_ev
        xslot_free[slot] = [last_kv, t_sp]
        sp_toks[ch] = t_sp

    if DEBUG:
        for ch in range(4):
            sch.dma("gpsimd", dbg["dbg_ln1"].ap()[ch], ln1sp[ch],
                    waits=[sp_toks[ch]], sem="DBH")
        sch.dma("gpsimd", dbg["dbg_kt"].ap(), kt[:], waits=[ktok[(NKVL - 1, 3)]],
                sem="DBH")
        sch.dma("gpsimd", dbg["dbg_v"].ap(), vsb[:], waits=[vtok[15]], sem="DBH")

    qt_tok = {}
    rr_free = [list(xslot_free[0]) + list(xslot_free[1]),
               list(xslot_free[0]) + list(xslot_free[1])]
    qtb_guard = list(xslot_free[1])
    for hg in range(4):
        t_scb = load_scales(qsc[:, hg * 512:(hg + 1) * 512], 512)
        q_deq_tok = None
        for ic in range(IC):
            slot = ic % 2
            ct = ctile[:, slot * 512:(slot + 1) * 512]
            t_cod = sch.dma("sync", ct, qcod[hg][ic * 128:(ic + 1) * 128, :],
                            waits=[ct_free[slot]], sem=f"CT{slot}")
            bcast(ic, t_scb, 512)
            q_deq_tok = dequant(wq[:, ic * 512:(ic + 1) * 512], t_cod, ct, 512,
                                extra=(([xslot_free[0][0], xslot_free[1][0]] + dbg_guard)
                                       if (hg == 0 and ic < 2) else ()))
            ct_free[slot] = q_deq_tok
        for ch in range(4):
            h_tok = [None, None]
            for half in range(2):
                inap = APc(ln1sp_t, (ch * 128 * 16384) + half * 8192,
                           [[16384, 128], [512, 16], [1, 512]])
                outap = APc(rr, half * 8192, [[16384, 128], [512, 16], [1, 512]])
                h_tok[half] = sch.dma("gpsimd", outap, inap,
                                      waits=rr_free[half] + [sp_toks[ch]],
                                      sem=f"RR{half}")
            last_g = None
            for hl in range(4):
                h = hg * 4 + hl
                bank = pC if hl % 2 == 0 else pD
                for ic in range(IC):
                    half = ic // 16
                    icr = ic % 16
                    last_g = mm(bank,
                                wq[:, ic * 512 + hl * 128: ic * 512 + hl * 128 + 128],
                                APc(rr, half * 8192 + icr * 512, [[16384, 128], [1, 512]]),
                                ic == 0, ic == IC - 1, waits=[q_deq_tok, h_tok[half]])
                rs = hl % 2
                qrb = bstg[:, rs * 512:(rs + 1) * 512]
                t_ev = rd("scalar", bank, lambda e, o=qrb, b=bank: e.copy(o, b.t[:]),
                          waits=[rope_scr[rs], sq_free[rs]])
                t_pm = mm(pF, c_perm[:], qrb, True, True, waits=[t_ev])
                t1s = bstg[:, (2 + rs) * 512:(3 + rs) * 512]
                cs = cosb[:, ch * 512:(ch + 1) * 512]
                sn = sinb[:, ch * 512:(ch + 1) * 512]
                t_t1 = sch.op("vector",
                              lambda e, o=t1s, a=qrb, c=cs: e.tensor_mul(o, a, c),
                              [t_ev, t_cosb, sq_free[2 + rs]], inc="V")
                if h < 8:
                    qdst = qta[:, h * 2048 + ch * 512: h * 2048 + ch * 512 + 512]
                else:
                    qdst = qtb[:, (h - 8) * 2048 + ch * 512: (h - 8) * 2048 + ch * 512 + 512]
                t_t2 = rd("vector", pF,
                          lambda e, o=qdst, b=sn: e.tensor_mul(o, pF.t[:], b),
                          waits=[t_sinb] + (list(qtb_guard) if h >= 8 else []))
                t_t3 = sch.op("vector", lambda e, o=qdst, a=t1s: e.tensor_add(o, o, a),
                              [t_t1, t_t2], inc="V")
                rope_scr[rs] = t_t3
                sq_free[rs] = [t_t1, t_pm]
                sq_free[2 + rs] = t_t3
                qt_tok[(h, ch)] = t_t3
            rr_free[0] = [last_g]
            rr_free[1] = [last_g]
            q_last_mm = last_g

    if DEBUG:
        sch.dma("gpsimd", dbg["dbg_qt"].ap()[:, 0:16384], qta[:],
                waits=[qt_tok[(7, 3)]], sem="DBG")
        t6_ = sch.dma("gpsimd", APc(dbg["dbg_qt"], 16384, [[NHL * S, 128], [1, 16384]]),
                      qtb[:], waits=[qt_tok[(15, 3)]], sem="DBG")
        qtb_guard = qtb_guard + [t6_]
        q_last_mm = q_last_mm

    sp_pt_tok = {}
    cps_free = [None] * 4
    pnb_free = [None] * 4
    trsc_free = [None] * 4
    bstg_guard = [rope_scr[0], rope_scr[1], sq_free[0], sq_free[1],
                  sq_free[2], sq_free[3]]
    fstg_gate0 = fstg_gate1 = None
    for g0 in range(0, NHL, 4):
        grp = list(range(g0, g0 + 4))
        t_nat = {}
        t_cp = {}
        t_mq = {}
        t_st = {}
        t_neg = {}
        t_exp = {}
        for h in grp:
            sl4 = h % 4
            qb0 = (qta[:, h * 2048: h * 2048 + 128] if h < 8
                   else qtb[:, (h - 8) * 2048: (h - 8) * 2048 + 128])
            t_nat[h] = mm(pE, qb0, kt[:, (h // 4) * 2048: (h // 4) * 2048 + 128],
                          sl4 == 0, True,
                          waits=[qt_tok[(h, 0)], ktok[(h // 4, 0)]],
                          out=pE.t[:, sl4 * 128:(sl4 + 1) * 128])
        for h in grp:
            sl4 = h % 4
            t_cp[h] = sch.op("scalar",
                             lambda e, o=fstg[:, (h % 4) * 128:(h % 4) * 128 + 128],
                             i=pE.t[:, sl4 * 128:(sl4 + 1) * 128]: e.copy(o, i),
                             [t_nat[h], cps_free[sl4]], inc="SC")
            pE.add_reader(t_cp[h])
        for h in grp:
            t_mq[h] = sch.op("vector",
                             lambda e, o=fstg[:, (h % 4) * 128:(h % 4) * 128 + 128]:
                             e.tensor_add(o, o, c_mq[:]),
                             [t_cp[h], tok_c], inc="V")
        for h in grp:
            sl4 = h % 4
            cw = fstg[:, sl4 * 128:sl4 * 128 + 128]
            tw = trsc[:, sl4 * 128:sl4 * 128 + 128]
            t_st[h] = sch.op("vector",
                             lambda e, o=tw, i=cw:
                             e.tensor_tensor(o[:, 0:64], i[:, 0:64],
                                             i[:, 64:128], ALU.max),
                             [t_mq[h], trsc_free[sl4]], inc="V")
        wdt, off = 32, 0
        while wdt >= 1:
            for h in grp:
                tw = trsc[:, (h % 4) * 128:(h % 4) * 128 + 128]
                t_st[h] = sch.op("vector",
                                 lambda e, o=tw, wd=wdt, of=off:
                                 e.tensor_tensor(o[:, of + 2 * wd:of + 3 * wd],
                                                 o[:, of:of + wd],
                                                 o[:, of + wd:of + 2 * wd],
                                                 ALU.max),
                                 [t_st[h]], inc="V")
            off += 2 * wdt
            wdt //= 2
        for h in grp:
            tw = trsc[:, (h % 4) * 128:(h % 4) * 128 + 128]
            t_neg[h] = sch.op("vector",
                              lambda e, o=tw:
                              e.tensor_scalar_mul(o[:, 127:128], o[:, 126:127],
                                                  c_cv[:, 9:10]),
                              [t_st[h]], inc="V")
        for h in grp:
            sl4 = h % 4
            t_exp[h] = sch.op("scalar",
                              lambda e, o=fstg[:, 512 + sl4 * 128:640 + sl4 * 128],
                              i=fstg[:, sl4 * 128:sl4 * 128 + 128],
                              b=trsc[:, sl4 * 128 + 127:sl4 * 128 + 128]:
                              e.activation(o, i, AF.Exp, bias=b,
                                           scale=c_cv[:, 1:2]),
                              [t_neg[h], pnb_free[sl4]], inc="SC")
            cps_free[sl4] = t_exp[h]
            trsc_free[sl4] = t_exp[h]
        for h in grp:
            sl4 = h % 4
            w_tr = [t_exp[h]] + pF.rtoks()
            t_tr = sch.op("tensor",
                          lambda e, i=fstg[:, 512 + sl4 * 128:640 + sl4 * 128]:
                          e.transpose(pF.t[0:128, 0:128], i, c_idf[:]),
                          w_tr, inc="T")
            pF.readers = {}
            pF.wtok = t_tr
            pnb_free[sl4] = t_tr
            sp_pt_tok[h] = rd("scalar", pF,
                              lambda e, o=bstg[:, h * 128:(h + 1) * 128]:
                              e.copy(o, pF.t[0:128, 0:128]),
                              waits=bstg_guard)
        if g0 == 12:
            fstg_gate0 = [t_exp[h] for h in grp]
            fstg_gate1 = [pnb_free[k] for k in range(4)]
    if DEBUG:
        t_dsp = sch.dma("gpsimd", dbg["dbg_sp"].ap(), bstg[:],
                        waits=[sp_pt_tok[h] for h in range(NHL)], sem="DBH")

    pt_free = [q_last_mm, q_last_mm]
    otl_tok = {}
    qta_guard = []
    cpy_tok = {}
    for h in range(NHL):
        kvh = h // 4
        for w in range(4):
            if h < 8:
                qb = qta[:, h * 2048 + w * 512: h * 2048 + w * 512 + 512]
            else:
                qb = qtb[:, (h - 8) * 2048 + w * 512: (h - 8) * 2048 + w * 512 + 512]
            it = h * 4 + w
            p = it % 2
            pbase = p * 8192
            sums = pA if p == 0 else pB
            ob = pG if p == 0 else pH
            KCW = 4 * (w + 1)
            qrdy = qt_tok[(h, w)]
            deferred = []

            def emit_cons(kc, toks, off, KCW=KCW, sums=sums, ob=ob, pbase=pbase,
                          kvh=kvh):
                ptsl = APc(pt, pbase + kc * 512 + off, [[16384, 128], [1, 512 - off]])
                mm(sums, c_ones[:], ptsl, kc == 0, kc == KCW - 1, waits=toks,
                   out=sums.t[:, off:512])
                mm(ob, vsb[:, kc * 512 + kvh * 128: kc * 512 + kvh * 128 + 128], ptsl,
                   kc == 0, kc == KCW - 1, waits=toks + [vtok[kc]],
                   out=ob.t[:, off:512])

            for kc in range(KCW):
                bank = pC if kc % 2 == 0 else pD
                jstar = kc - 4 * w
                off = jstar * 128 if jstar > 0 else 0
                t_ms = mm(bank, kt[:, kvh * 2048 + kc * 128: kvh * 2048 + kc * 128 + 128],
                          qb[:, off:512], True, True,
                          waits=[qrdy, ktok[(kvh, kc // 4)], pt_free[p]],
                          out=bank.t[:, off:512])
                toks = []
                if jstar < 0:
                    toks.append(rd("scalar", bank,
                                   lambda e, b=bank,
                                   o=APc(pt, pbase + kc * 512, [[16384, 128], [1, 512]]):
                                   e.activation(o, b.t[:], AF.Exp, bias=c_cv[:, 0:1],
                                                 scale=c_cv[:, 1:2])))
                else:
                    if w == 0 and kc == 0:
                        tok_e = sch.op("scalar",
                                       lambda e,
                                       o=APc(pt, pbase, [[16384, 128], [1, 128]]),
                                       i=bstg[:, h * 128:(h + 1) * 128]:
                                       e.copy(o, i),
                                       [sp_pt_tok[h], pt_free[p]], inc="SC")
                        cpy_tok[h] = tok_e
                        toks.append(tok_e)
                        toks.append(rd("scalar", bank,
                                       lambda e, b=bank,
                                       o=APc(pt, pbase + 128, [[16384, 128], [1, 384]]):
                                       e.activation(o, b.t[:, 128:512], AF.Exp,
                                                     bias=c_cv[:, 0:1], scale=c_cv[:, 1:2])))
                    else:
                        t_madd = sch.op("vector",
                                        lambda e, b=bank, j=jstar:
                                        e.tensor_add(b.t[:, j * 128:(j + 1) * 128],
                                                     b.t[:, j * 128:(j + 1) * 128], c_mT[:]),
                                        [t_ms], inc="V")
                        bank.add_reader(t_madd)
                        toks.append(rd("scalar", bank,
                                       lambda e, b=bank, j=jstar,
                                       o=APc(pt, pbase + kc * 512 + jstar * 128,
                                             [[16384, 128], [1, (4 - jstar) * 128]]):
                                       e.activation(o, b.t[:, j * 128:512], AF.Exp,
                                                     bias=c_cv[:, 0:1], scale=c_cv[:, 1:2]),
                                       waits=[t_madd]))
                deferred.append((kc, toks, off))
                if len(deferred) > 1:
                    kcp, tks, offp = deferred.pop(0)
                    emit_cons(kcp, tks, offp)
            while deferred:
                kcp, tks, offp = deferred.pop(0)
                emit_cons(kcp, tks, offp)
            t_l1 = rd("scalar", sums, lambda e, s=sums: e.activation(m_sb[:], s.t[:],
                                                                     AF.Ln,
                                                                     scale=c_cv[:, 7:8]))
            t_rc = sch.op("scalar", lambda e: e.activation(m_sb[:], m_sb[:], AF.Exp,
                                                           scale=c_cv[:, 6:7],
                                                           bias=c_cv[:, 8:9]),
                          [t_l1], inc="SC")
            odst = APc(otl, h * 2048 + w * 512, [[32768, 128], [1, 512]])
            guard = []
            if h == 0 and w == 0:
                guard = [q_last_mm] + (dbg_guard if DEBUG else [])
            if h == 8 and w == 0:
                guard = list(qta_guard) + (qtb_guard[2:] if DEBUG else [])
            t_oe = rd("vector", ob,
                      lambda e, o=odst, b=ob: e.tensor_mul(o, b.t[:], m_sb[:]),
                      waits=[t_rc] + guard)
            otl_tok[(h, w)] = t_oe
            pt_free[p] = ob.wtok
        if h == 7:
            qta_guard = [pt_free[0], pt_free[1]]

    if DEBUG:
        dbg_ag_tok = sch.dma("gpsimd", dbg["dbg_ag"].ap(), otl[:],
                             waits=[otl_tok[(15, 3)]], sem="DBH")

    ocod_free = [[pt_free[0], pt_free[1]], [pt_free[0], pt_free[1]]]
    wo_free = [[pt_free[0], pt_free[1]] + dbg_guard,
               [pt_free[0], pt_free[1]] + dbg_guard]
    sq_free = [[rope_scr[0]] + [cpy_tok[j] for j in range(0, 4)],
               [rope_scr[1]] + [cpy_tok[j] for j in range(4, 8)],
               [sq_free[2]] + [cpy_tok[j] for j in range(8, 12)],
               [sq_free[3]] + [cpy_tok[j] for j in range(12, 16)]]
    if DEBUG:
        sq_free = [sq_free[k] + [t_dsp] for k in range(4)]
    rsin_tok = [None] * 4
    for mb in range(16):
        bs = mb % 2
        cd = []
        for g8 in range(2):
            inap = APc(ocod.tensor, mb * 2048 * 256 + (g8 * 8 * 128) * 256,
                       [[256, 128], [128 * 256, 8], [1, 256]])
            outap = APc(ocod_sb, bs * 4096 + g8 * 8 * 256,
                        [[8192, 128], [256, 8], [1, 256]])
            cd.append(sch.dma("sync", outap, inap, waits=ocod_free[bs],
                               sem=f"OC{bs}"))
        t_scb = load_scales(osc[:, mb * 256:(mb + 1) * 256], 256, rows=16)
        dq = None
        for icl in range(16):
            bcast(icl, t_scb, 256, rows=16)
            dq = dequant(wo[:, bs * 4096 + icl * 256: bs * 4096 + icl * 256 + 256],
                         cd[1],
                         APc(ocod_sb, bs * 4096 + icl * 256, [[8192, 128], [1, 256]]),
                         256, extra=wo_free[bs])
        ocod_free[bs] = [dq]
        last_g = None
        for mo in range(2):
            m = mb * 2 + mo
            for tcg in range(4):
                bank = pC if (mo * 4 + tcg) % 2 == 0 else pD
                for icl in range(16):
                    last_g = mm(bank,
                                APc(wo, bs * 4096 + icl * 256 + mo * 128,
                                    [[8192, 128], [1, 128]]),
                                APc(otl, icl * 2048 + tcg * 512, [[32768, 128], [1, 512]]),
                                icl == 0, icl == 15,
                                waits=[dq, otl_tok[(icl, tcg)]])
                sl = (m * 4 + tcg) % 4
                ost = bstg[:, sl * 512:(sl + 1) * 512]
                t_ev = rd("scalar", bank, lambda e, o=ost, b=bank: e.copy(o, b.t[:]),
                          waits=[sq_free[sl]])
                tok = sch.dma("gpsimd",
                              rs_in[tcg // 2][m][:, (tcg % 2) * 512:(tcg % 2) * 512 + 512],
                              ost, waits=[t_ev], sem=f"RSI{sl}")
                sq_free[sl] = tok
                rsin_tok[sl] = tok
        wo_free[bs] = [last_g]

    t_cc = sch.op("gpsimd",
                  lambda e: e.collective_compute(
                      "ReduceScatter", ALU.add,
                      replica_groups=[[0, 1], [2, 3], [4, 5], [6, 7]],
                      ins=[rs_in.opt()], outs=[rs_out.opt()]),
                  waits=list(rsin_tok), inc="CC")
    if DEBUG:
        sch.dma("gpsimd", dbg["dbg_outt"].ap(), rs_out, waits=[t_cc], sem="DBH")
    statbanks = {0: (pA, pB), 1: (pF, pH)}

    def ln_finish_var(sB, rb_slot):
        rbs = rb_sb[:, rb_slot * 512:(rb_slot + 1) * 512]
        t_ln = rd("scalar", sB, lambda e, s=sB, o=rbs:
                  e.activation(o, s.t[:], AF.Ln, bias=c_cv[:, 2:3]))
        return sch.op("scalar", lambda e, o=rbs: e.activation(o, o, AF.Exp,
                                                              scale=c_cv[:, 5:6]),
                      [t_ln], inc="SC"), rbs

    outt_guard = wo_free[0] + wo_free[1] + [t_cc] + \
        ([dbg_ag_tok] if DEBUG else [])
    t_outt = None
    for m in range(32):
        t_outt = sch.dma("gpsimd", APc(outt, m * 1024, [[32768, 128], [1, 1024]]),
                         rs_out[m], waits=outt_guard, sem="OT")
    y_tok = {}
    y3_tok = {}
    sqtok = {}
    fslot = [fstg_gate0, fstg_gate1]
    for tc in range(2):
        sA, sB = statbanks[tc]
        for m in range(32):
            osl = APc(outt, m * 1024 + tc * 512, [[32768, 128], [1, 512]])
            mm(sA, c_avg[:], osl, m == 0, m == 31, waits=[t_outt])
        for m in range(32):
            sl = m % 2
            osl = APc(outt, m * 1024 + tc * 512, [[32768, 128], [1, 512]])
            t_1 = sch.op("vector",
                         lambda e, o=osl, s=sA: e.tensor_sub(o, o, s.t[:]),
                         [sA.wtok], inc="V")
            sA.add_reader(t_1)
            sqs = bstg[:, sl * 512:(sl + 1) * 512]
            t_sq = sch.op("scalar",
                          lambda e, o=sqs, i=osl: e.activation(o, i, AF.Square),
                          [t_1, sq_free[sl]], inc="SC")
            sq_free[sl] = None
            tb = mm(sB, c_avg[:], sqs, m == 0, m == 31, waits=[t_sq], sig=True)
            sq_free[sl] = tb
            sqtok[(m, tc)] = t_sq
        t_rb, rbs = ln_finish_var(sB, tc)
        for m in range(32):
            sl = m % 2
            osl = APc(outt, m * 1024 + tc * 512, [[32768, 128], [1, 512]])
            xtile = fstg[:, sl * 512:(sl + 1) * 512]
            t_x = sch.dma("sync", xtile,
                          APc(xTo.tensor, m * 128 * TOWN + tc * 512, [[TOWN, 128], [1, 512]]),
                          waits=[fslot[sl]], sem=f"F{sl}")
            t_2 = sch.op("vector",
                         lambda e, o=osl, r=rbs: e.tensor_mul(o, o, r),
                         [t_rb, sqtok[(m, tc)]], inc="V")
            t_3 = sch.op("vector", lambda e, o=osl, x_=xtile: e.tensor_add(o, o, x_),
                         [t_2, t_x], inc="V")
            fslot[sl] = t_3
            y3_tok[(m, tc)] = t_3
            y_tok[(m, tc)] = sch.dma("gpsimd", ysp[m][:, tc * 512:(tc + 1) * 512],
                                     osl, waits=[t_3], sem=f"YW{2 + sl}")
    if DEBUG:
        sch.dma("gpsimd", dbg["dbg_y"].ap(), ysp, waits=[y_tok[(31, 1)], y_tok[(30, 1)]],
                sem="DBH")

    smalls = []
    for dst, src, cols in ((wqfb, WqfT, 256), (wkfb, WkfT, 256), (atb, ATt, 128)):
        for g8 in range(4):
            inap = APc(src.tensor, (g8 * 8 * 128) * cols,
                       [[cols, 128], [128 * cols, 8], [1, cols]])
            outap = APc(dst, g8 * 8 * cols, [[IC * cols, 128], [cols, 8], [1, cols]])
            smalls.append(sch.dma("gpsimd", outap, inap, waits=wo_free[0] + wo_free[1],
                                  sem="SM"))
    smalls.append(sch.dma("gpsimd", bmtb[:], BmTt, sem="SM"))
    t_smalls = smalls[-1]

    ln2_tok = {}
    for tc in range(2):
        sA, sB = statbanks[tc]
        for m in range(32):
            osl = APc(outt, m * 1024 + tc * 512, [[32768, 128], [1, 512]])
            mm(sA, c_avg[:], osl, m == 0, m == 31, waits=[y3_tok[(m, tc)]])
        for m in range(32):
            sl = m % 2
            osl = APc(outt, m * 1024 + tc * 512, [[32768, 128], [1, 512]])
            dst = APc(ln2, m * 1024 + tc * 512, [[32768, 128], [1, 512]])
            t_1 = sch.op("vector",
                         lambda e, o=dst, a=osl, s=sA: e.tensor_sub(o, a, s.t[:]),
                         [ln2_tok.get((m - 2, tc)), sA.wtok], inc="V")
            sA.add_reader(t_1)
            sqs = bstg[:, (2 + sl) * 512:(3 + sl) * 512]
            t_sq = sch.op("scalar",
                          lambda e, o=sqs, i=dst: e.activation(o, i, AF.Square),
                          [t_1, sq_free[2 + sl]], inc="SC")
            tb = mm(sB, c_avg[:], sqs, m == 0, m == 31, waits=[t_sq], sig=True)
            sq_free[2 + sl] = tb
            ln2_tok[(m, tc)] = t_1
        t_rb, rbs = ln_finish_var(sB, tc)
        l3 = APc(ln2, tc * 512, [[32768, 128], [1024, 32], [1, 512]])
        rbc2 = APc(rb_sb, tc * 512, [[1024, 128], [0, 32], [1, 512]])
        t_nm2 = sch.op("vector", lambda e, a=l3, r=rbc2: e.tensor_mul(a, a, r),
                       [t_rb, ln2_tok[(31, tc)], sq_free[2], sq_free[3]], inc="V")
        for m in range(32):
            ln2_tok[(m, tc)] = t_nm2
    if DEBUG:
        sch.dma("gpsimd", dbg["dbg_ln2"].ap(), ln2[:], waits=[ln2_tok[(31, 1)]],
                sem="DBH")

    def ln2ap(ic, tc):
        return APc(ln2, ic * 1024 + tc * 512, [[32768, 128], [1, 512]])

    ln2_all = ln2_tok[(31, 1)]
    qf_toks = []
    for nb in range(2):
        for tc in range(2):
            bank = pC if (nb * 2 + tc) % 2 == 0 else pD
            for ic in range(IC):
                mm(bank, APc(wqfb, ic * 256 + nb * 128, [[8192, 128], [1, 128]]),
                   ln2ap(ic, tc), ic == 0, ic == IC - 1,
                   waits=[t_smalls, ln2_tok[(ic, tc)]])
            qf_toks.append(rd("scalar", bank,
                              lambda e, o=qf[:, nb * 1024 + tc * 512: nb * 1024 + tc * 512 + 512],
                              b=bank: e.copy(o, b.t[:])))
            bank2 = pD if (nb * 2 + tc) % 2 == 0 else pC
            for ic in range(IC):
                mm(bank2, APc(wkfb, ic * 256 + nb * 128, [[8192, 128], [1, 128]]),
                   ln2ap(ic, tc), ic == 0, ic == IC - 1, waits=[t_smalls])
            qf_toks.append(rd("scalar", bank2,
                              lambda e, o=kf[:, nb * 1024 + tc * 512: nb * 1024 + tc * 512 + 512],
                              b=bank2: e.copy(o, b.t[:])))
    t_qk = sch.op("vector", lambda e: e.tensor_mul(qk[:], qf[:], kf[:]),
                  qf_toks, inc="V")
    wexp_tok = {}
    tiny_chain = None
    for tc in range(2):
        for nb in range(2):
            seg = c_seg0 if nb == 0 else c_seg1
            mm(pE, seg[:], APc(qk, nb * 1024 + tc * 512, [[2048, 128], [1, 512]]),
               nb == 0, nb == 1, waits=[t_qk], out=pE.t[0:8, 0:512])
        t_sce = rd("scalar", pE, lambda e: e.copy(sce[:], pE.t[0:8, 0:512]))
        tr_toks = []
        for ts in range(4):
            w_ = list(pG.rtoks()) + [t_sce]
            tok = sch.op("tensor",
                         lambda e, i=sce[:, ts * 128:(ts + 1) * 128]:
                         e.transpose(pG.t[0:128, 0:8], i, c_idf[0:8, 0:8]),
                         w_, inc="T")
            pG.readers = {}
            pG.wtok = tok
            tr_toks.append(rd("scalar", pG,
                              lambda e, o=snat[:, ts * 8:(ts + 1) * 8]:
                              e.copy(o, pG.t[0:128, 0:8])))
        wn_toks = []
        for ts in range(4):
            t_ex = sch.op("scalar",
                          lambda e, i=snat[:, ts * 8:(ts + 1) * 8],
                          o=wn[:, ts * 8:(ts + 1) * 8]:
                          e.activation(o, i, AF.Exp, scale=c_cv[:, 3:4]),
                          [tr_toks[ts], tiny_chain], inc="SC")
            t_sm = sch.op("vector",
                          lambda e, i=wn[:, ts * 8:(ts + 1) * 8]:
                          e.reduce_sum(tiny[:, 4:5], i, AX.X), [t_ex], inc="V")
            t_l1 = sch.op("scalar",
                          lambda e: e.activation(tiny[:, 5:6], tiny[:, 4:5], AF.Ln),
                          [t_sm], inc="SC")
            t_rc = sch.op("scalar",
                          lambda e: e.activation(tiny[:, 5:6], tiny[:, 5:6], AF.Exp,
                                                 scale=c_cv[:, 6:7]),
                          [t_l1], inc="SC")
            t_n = sch.op("vector",
                         lambda e, o=wn[:, ts * 8:(ts + 1) * 8]:
                         e.tensor_scalar_mul(o, o, tiny[:, 5:6]), [t_rc], inc="V")
            tiny_chain = t_n
            wn_toks.append(t_n)
        for ts in range(4):
            w_ = list(pG.rtoks()) + [wn_toks[ts]]
            tok = sch.op("tensor",
                         lambda e, i=wn[:, ts * 8:(ts + 1) * 8]:
                         e.transpose(pG.t[0:8, 0:128], i, c_idf[:]),
                         w_, inc="T")
            pG.readers = {}
            pG.wtok = tok
            rd("scalar", pG,
               lambda e, o=w8[:, ts * 128:(ts + 1) * 128]: e.copy(o, pG.t[0:8, 0:128]))
        mm(pG, c_e8[:], w8[:], True, True)
        wexp_tok[tc] = rd("scalar", pG,
                          lambda e, o=wexp[:, tc * 512:(tc + 1) * 512]: e.copy(o, pG.t[:]))
    if DEBUG:
        sch.dma("gpsimd", dbg["dbg_qf"].ap(), qf[:], waits=qf_toks, sem="DBH")
        sch.dma("gpsimd", dbg["dbg_kf"].ap(), kf[:], waits=qf_toks, sem="DBH")
        sch.dma("gpsimd", dbg["dbg_sn"].ap()[:, 0:32], snat[:], waits=[tiny_chain],
                sem="DBH")
        sch.dma("gpsimd", dbg["dbg_sn"].ap()[:, 32:64], wn[:], waits=[tiny_chain],
                sem="DBH")
        sch.dma("gpsimd", dbg["dbg_w8"].ap(), w8[:], waits=[wexp_tok[1]], sem="DBH")
        sch.dma("gpsimd", dbg["dbg_wexp"].ap(), wexp[:], waits=[wexp_tok[1]],
                sem="DBH")
    g_tok = {}
    for tc in range(2):
        bank = pC if tc == 0 else pD
        for ic in range(IC):
            mm(bank, APc(atb, ic * 128, [[4096, 128], [1, 128]]), ln2ap(ic, tc),
               ic == 0, ic == IC - 1, waits=[t_smalls])
        g_tok[tc] = rd("vector", bank,
                       lambda e, o=gsb[:, tc * 512:(tc + 1) * 512], b=bank,
                       wv_=wexp[:, tc * 512:(tc + 1) * 512]:
                       e.tensor_mul(o, b.t[:], wv_), waits=[wexp_tok[tc]])

    if DEBUG:
        sch.dma("gpsimd", dbg["dbg_g"].ap(), gsb[:], waits=[g_tok[0], g_tok[1]],
                sem="DBH")

    z_tok = {}
    wbufs = [wblk, wblk2]
    wfree = {0: [g_tok[0], g_tok[1], t_qk],
             1: [g_tok[0], g_tok[1], t_qk] + ([dbg_ag_tok] if DEBUG else [])}
    zslot = [sq_free[0], sq_free[1]]
    for blk_i in range(8):
        wb = wbufs[blk_i % 2]
        wd = []
        for g8 in range(4):
            inap = APc(WT.tensor, (g8 * 8 * 128) * D + blk_i * 512,
                       [[D, 128], [128 * D, 8], [1, 512]])
            outap = APc(wb, g8 * 8 * 512, [[16384, 128], [512, 8], [1, 512]])
            wd.append(sch.dma("gpsimd", outap, inap, waits=wfree[blk_i % 2],
                              sem=f"WB{(blk_i % 2) * 4 + g8}"))
        last_g = None
        for mo in range(4):
            m = blk_i * 4 + mo
            for tc in range(2):
                bank = pC if (mo * 2 + tc) % 2 == 0 else pD
                for ic in range(IC):
                    mm(bank, APc(wb, ic * 512 + mo * 128, [[16384, 128], [1, 128]]),
                       ln2ap(ic, tc), ic == 0, False, waits=[wd[ic // 8], ln2_all])
                last_g = mm(bank,
                            bmtb[:, blk_i * 512 + mo * 128: blk_i * 512 + mo * 128 + 128],
                            APc(gsb, tc * 512, [[1024, 128], [1, 512]]),
                            False, True, waits=[g_tok[tc], t_smalls])
                sl = (m * 2 + tc) % 2
                zst = bstg[:, sl * 512:(sl + 1) * 512]
                t_ev = rd("scalar", bank,
                          lambda e, o=zst, b=bank, mi=m:
                          e.activation(o, b.t[:], AF.Identity, bias=c_b[:, mi:mi + 1]),
                          waits=[zslot[sl], z_tok.get((m - 1, tc))])
                sA, sB = statbanks[tc]
                tb = mm(sA, c_avg[:], zst, m == 0, m == 31, waits=[t_ev], sig=True)
                zslot[sl] = tb
                z_tok[(m, tc)] = sch.dma("gpsimd", zsp[m][:, tc * 512:(tc + 1) * 512],
                                         zst, waits=[t_ev, tb], sem=f"ZW{tc}")
        wfree[blk_i % 2] = [last_g]
    if DEBUG:
        sch.dma("gpsimd", dbg["dbg_z"].ap(), zsp,
                waits=[z_tok[(31, 0)], z_tok[(31, 1)]], sem="DBH")

    oslot = [None] * 2
    zslot = [z_tok[(31, 0)], z_tok[(31, 1)]]
    ln2_dead = [wfree[0], wfree[1]]
    sqtok3 = {}
    for tc in range(2):
        sA, sB = statbanks[tc]
        for m in range(32):
            sl = m % 2
            ztile = bstg[:, sl * 512:(sl + 1) * 512]
            t_z = sch.dma("gpsimd", ztile, zsp[m][:, tc * 512:(tc + 1) * 512],
                          waits=[z_tok[(m, tc)], zslot[sl]], sem=f"B{sl}")
            dsl = APc(ln2, m * 1024 + tc * 512, [[32768, 128], [1, 512]])
            t_1 = sch.op("vector",
                         lambda e, o=dsl, a=ztile, s=sA: e.tensor_sub(o, a, s.t[:]),
                         [t_z, sA.wtok] + ln2_dead, inc="V")
            sA.add_reader(t_1)
            sqs = bstg[:, sl * 512:(sl + 1) * 512]
            t_sq = sch.op("scalar", lambda e, o=sqs, i=dsl: e.activation(o, i, AF.Square),
                          [t_1], inc="SC")
            tb = mm(sB, c_avg[:], sqs, m == 0, m == 31, waits=[t_sq], sig=True)
            zslot[sl] = tb
            sqtok3[(m, tc)] = t_sq
        t_rb, rbs = ln_finish_var(sB, tc)
        for m in range(32):
            sl = m % 2
            dsl = APc(ln2, m * 1024 + tc * 512, [[32768, 128], [1, 512]])
            ytile = bstg[:, (2 + sl) * 512:(3 + sl) * 512]
            t_y = sch.dma("gpsimd", ytile, ysp[m][:, tc * 512:(tc + 1) * 512],
                          waits=[sq_free[2 + sl], y_tok[(30, 1)], y_tok[(31, 1)]],
                          sem=f"B{2 + sl}")
            sq_free[2 + sl] = None
            otile = fstg[:, sl * 512:(sl + 1) * 512]
            t_2 = sch.op("vector",
                         lambda e, o=otile, a=dsl, r=rbs: e.tensor_mul(o, a, r),
                         [t_rb, sqtok3[(m, tc)], oslot[sl]], inc="V")
            t_3 = sch.op("vector", lambda e, o=otile, y_=ytile: e.tensor_add(o, o, y_),
                         [t_2, t_y], inc="V")
            sq_free[2 + sl] = t_3
            oslot[sl] = sch.dma("sync",
                                APc(out_t, m * 128 * TOWN + tc * 512,
                                    [[TOWN, 128], [1, 512]]),
                                otile, waits=[t_3], sem=f"F{sl}")

    finals = [(sm, sch.cnt[sm]) for sm in sorted(sch.dma_sems)] +              [("CC", sch.cnt["CC"])]
    sch.op("sync", lambda e: None, finals)
    sch.op("gpsimd", lambda e: None, finals)

    sems = {n: st.enter_context(nc.semaphore(n))
            for n in sorted(set(sch.cnt) | {"T", "V", "SC"})}
    with nc.Block() as blkk:
        def make_body(eng):
            def body(e):
                for wl, fn, inc, n in sch.q[eng]:
                    for s_, v_ in wl:
                        e.wait_ge(sems[s_], v_)
                    inst = fn(e)
                    if inc is not None:
                        inst.then_inc(sems[inc], n)
            return body
        blkk.sync(make_body("sync"))
        blkk.gpsimd(make_body("gpsimd"))
        blkk.tensor(make_body("tensor"))
        blkk.vector(make_body("vector"))
        blkk.scalar(make_body("scalar"))

    st.close()
    return nc


def _consts():
    bf = ml_dtypes.bfloat16
    avg = np.full((128, 128), 1.0 / 4096.0, np.float32).astype(bf)
    ones = np.ones((128, 128), np.float32).astype(bf)
    ident = np.eye(128, dtype=np.float32)
    perm = np.zeros((128, 128), np.float32)
    for i in range(128):
        perm[i ^ 1, i] = 1.0
    rk, rq = np.meshgrid(np.arange(128), np.arange(128), indexing="ij")
    maskT = np.where(rk <= rq, 0.0, -1e9).astype(np.float32)
    maskq = maskT.T.copy()
    seg0 = np.zeros((128, 8), np.float32)
    seg1 = np.zeros((128, 8), np.float32)
    for p in range(128):
        seg0[p, p // 32] = 1.0
        seg1[p, 4 + p // 32] = 1.0
    e8 = np.zeros((8, 128), np.float32)
    for e_ in range(8):
        e8[e_, e_ * 16:(e_ + 1) * 16] = 1.0
    cv = np.zeros((128, 12), np.float32)
    cv[:, 0] = SM_SHIFT
    cv[:, 1] = SM_SCALE
    cv[:, 2] = EPS
    cv[:, 3] = E_SCALE
    cv[:, 4] = -10.0
    cv[:, 5] = -0.5
    cv[:, 6] = -1.0
    cv[:, 7] = 2.0 ** -6
    cv[:, 8] = -6.0 * np.log(2.0)
    cv[:, 9] = -SM_SCALE
    sel = np.zeros((32, 4096), np.float32)
    for g in range(32):
        sel[g, g * 128:(g + 1) * 128] = 1.0
    pos = np.arange(S, dtype=np.float64)
    invf = 1.0 / (10000.0 ** (np.arange(0, HD, 2, dtype=np.float64) / HD))
    f = pos[None, :] * invf[(np.arange(128) // 2), None]
    cos = np.cos(f).astype(np.float32)
    sgn = np.where(np.arange(128) % 2 == 0, -1.0, 1.0)
    sin = (np.sin(f) * sgn[:, None]).astype(np.float32)
    return dict(avgb=avg, onesb=ones, identf=ident, epermb=perm.astype(bf),
                maskTb=maskT.astype(bf), maskqb=maskq.astype(bf),
                seg0=seg0.astype(bf), seg1=seg1.astype(bf),
                exp8f=e8.astype(bf), constv=cv, selb=sel.astype(bf), cosf=cos, sinf=sin)


def kernel(L0_attn_hidden_states, L0_attn_q_codes, L0_attn_q_scales,
           L0_attn_k_codes, L0_attn_k_scales, L0_attn_v_codes, L0_attn_v_scales,
           L0_attn_o_codes, L0_attn_o_scales, L0_ffn_W, L0_ffn_b,
           L0_ffn_Wq, L0_ffn_Wk, L0_ffn_A, L0_ffn_Bm):
    x = np.asarray(L0_attn_hidden_states, np.float32)
    cons = _consts()
    WTh = np.ascontiguousarray(np.asarray(L0_ffn_W, np.float32).T)
    WqfTh = np.ascontiguousarray(np.asarray(L0_ffn_Wq, np.float32).T)
    WkfTh = np.ascontiguousarray(np.asarray(L0_ffn_Wk, np.float32).T)
    ATh = np.ascontiguousarray(
        np.asarray(L0_ffn_A, np.float32).transpose(2, 0, 1).reshape(D, 128))
    BmTh = np.ascontiguousarray(
        np.asarray(L0_ffn_Bm, np.float32).transpose(0, 2, 1).reshape(128, D))
    bhh = np.ascontiguousarray(np.asarray(L0_ffn_b, np.float32).reshape(32, 128).T)
    ocodes_full = np.asarray(L0_attn_o_codes, np.int32)
    osc_full = np.asarray(L0_attn_o_scales, np.float32)
    ocodh = {}
    osch = {}
    for hh_ in range(2):
        sl_ = ocodes_full[hh_ * 2048:(hh_ + 1) * 2048]
        ocodh[hh_] = np.ascontiguousarray(
            sl_.reshape(2048, 16, 256).transpose(1, 0, 2))
        osch[hh_] = np.ascontiguousarray(osc_full[hh_ * 16:(hh_ + 1) * 16])
    qcodes = np.asarray(L0_attn_q_codes, np.int32)
    kcodes = np.asarray(L0_attn_k_codes, np.int32)
    vcodes = np.asarray(L0_attn_v_codes, np.int32)

    in_maps = []
    for c in range(8):
        b, hh = c // 2, c % 2
        qslice = qcodes[:, hh * 2048:(hh + 1) * 2048]
        m = dict(cons)
        m.update(
            xT=np.ascontiguousarray(x[b].T),
            xTo=np.ascontiguousarray(x[b, hh * TOWN:(hh + 1) * TOWN].T),
            qcod=np.ascontiguousarray(qslice.reshape(D, 4, 512).transpose(1, 0, 2)),
            kcod=np.ascontiguousarray(kcodes[:, hh * 512:(hh + 1) * 512]),
            vcod=np.ascontiguousarray(vcodes[:, hh * 512:(hh + 1) * 512]),
            ocod=ocodh[hh],
            qsc=np.ascontiguousarray(
                np.asarray(L0_attn_q_scales, np.float32)[:, hh * 2048:(hh + 1) * 2048]),
            ksc=np.ascontiguousarray(
                np.asarray(L0_attn_k_scales, np.float32)[:, hh * 512:(hh + 1) * 512]),
            vsc=np.ascontiguousarray(
                np.asarray(L0_attn_v_scales, np.float32)[:, hh * 512:(hh + 1) * 512]),
            osc=osch[hh], WT=WTh, WqfT=WqfTh, WkfT=WkfTh, ATt=ATh, BmTt=BmTh, bh=bhh,
        )
        in_maps.append(m)

    nc = build_nc()
    res = run_bass_kernel_spmd(nc, in_maps, core_ids=list(range(8)))

    outp = np.empty((4, S, D), np.float32)
    for c in range(8):
        b, hh = c // 2, c % 2
        outp[b, hh * TOWN:(hh + 1) * TOWN, :] = \
            np.asarray(res.results[c]["outT"], np.float32).T
    if DEBUG:
        kernel._dbg = [res.results[c] for c in range(8)]
    return outp
